# version 15 (final)
# baseline (speedup 1.0000x reference)
import numpy as np
import ml_dtypes
from contextlib import ExitStack

import concourse.bass as bass
import concourse.mybir as mybir
from concourse.bass import IndirectOffsetOnAxis
from concourse.bass_utils import run_bass_kernel_spmd

F32 = mybir.dt.float32
BF16 = mybir.dt.bfloat16
U8 = mybir.dt.uint8
U32 = mybir.dt.uint32
I32 = mybir.dt.int32
AF = mybir.ActivationFunctionType
ALU = mybir.AluOpType
AX = mybir.AxisListType

NCORES = 8
B, T, C = 2, 8192, 1024
TCORE = 2048
HALO = 256
NTT = 16
NTQ = 18
NH, HD = 16, 64
W3 = 384
FF = 4096
NE = 8
NKC = 8
NEG = -1.0e30
NSL = 8
EPS1 = 1e-5
EPS2 = 1e-12

ENGINES = ("sync", "gpsimd", "scalar", "vector", "tensor")


class Sched:

    def __init__(self):
        self.blocks = []
        self.cur = None

    def new_block(self):
        self.cur = []
        self.blocks.append(self.cur)

    def op(self, eng, fn, waits=()):
        rec = {"eng": eng, "fn": fn, "waits": [w for w in waits if w is not None],
               "inc": None}
        self.cur.append(rec)
        return rec

    def finalize(self):
        for blk in self.blocks:
            for rec in blk:
                for wtg in rec["waits"]:
                    if isinstance(wtg, dict) and wtg["inc"] is None:
                        wtg["inc"] = -1
        counters = {e: 0 for e in ENGINES}
        for blk in self.blocks:
            for rec in blk:
                if rec["inc"] == -1:
                    counters[rec["eng"]] += 1
                    rec["inc"] = counters[rec["eng"]]

    def emit_block(self, blk_ops, eng_name, engine, eng_sems):
        highwater = {}
        for rec in blk_ops:
            if rec["eng"] != eng_name:
                continue
            for wtg in rec["waits"]:
                if isinstance(wtg, dict):
                    sem, val, key = eng_sems[wtg["eng"]], wtg["inc"], ("e", wtg["eng"])
                else:
                    sem, val = wtg
                    key = ("s", id(sem))
                if highwater.get(key, -1) >= val:
                    continue
                highwater[key] = val
                engine.wait_ge(sem, val)
            inst = rec["fn"](engine)
            if rec["inc"] is not None:
                assert inst is not None, f"wait-target op returned no inst on {eng_name}"
                inst.then_inc(eng_sems[rec["eng"]], 1)


def build_program():
    nc = bass.Bass("TRN2", target_bir_lowering=False)

    d_x = nc.dram_tensor("x_ext", [NTQ * 128, C], F32, kind="ExternalInput").ap()
    d_wa = nc.dram_tensor("w_attn", [C, 3 * C], F32, kind="ExternalInput").ap()
    d_ba = nc.dram_tensor("b_attn", [3 * C], F32, kind="ExternalInput").ap()
    d_wp = nc.dram_tensor("w_proj", [C, C], F32, kind="ExternalInput").ap()
    d_bp = nc.dram_tensor("b_proj", [C], F32, kind="ExternalInput").ap()
    d_wrt = nc.dram_tensor("wr_t", [C, NE], F32, kind="ExternalInput").ap()
    d_w1 = nc.dram_tensor("w1", [NE, C, FF], F32, kind="ExternalInput").ap()
    d_b1 = nc.dram_tensor("b1", [NE, FF], F32, kind="ExternalInput").ap()
    d_wo = nc.dram_tensor("wo", [FF, C], F32, kind="ExternalInput").ap()
    d_bo = nc.dram_tensor("bo", [C], F32, kind="ExternalInput").ap()
    d_gam = nc.dram_tensor("gamma", [C], F32, kind="ExternalInput").ap()
    d_bet = nc.dram_tensor("beta", [C], F32, kind="ExternalInput").ap()
    d_msk = nc.dram_tensor("masks", [3, 128, W3], F32, kind="ExternalInput").ap()
    d_out = nc.dram_tensor("out", [TCORE, C], F32, kind="ExternalOutput").ap()
    d_x1 = nc.dram_tensor("x1_scratch", [TCORE, C], F32, kind="Internal").ap()
    d_idx = nc.dram_tensor("idx_scratch", [TCORE], F32, kind="Internal").ap()
    d_eids = nc.dram_tensor("eids", [NE, 1], F32, kind="ExternalInput").ap()
    d_ones8 = nc.dram_tensor("ones8", [NE, 1], F32, kind="ExternalInput").ap()
    d_xg = nc.dram_tensor("xg_scratch", [NE * 512, C], BF16, kind="Internal").ap()
    d_yg = nc.dram_tensor("yg_scratch", [NE * 512, C], F32, kind="Internal").ap()
    d_dst = nc.dram_tensor("dst_scratch", [TCORE], F32, kind="Internal").ap()
    d_wob = nc.dram_tensor("wo_bf", [FF, C], BF16, kind="Internal").ap()

    ident_bf = nc.inline_tensor(np.eye(128, dtype=ml_dtypes.bfloat16), "ident_bf").ap()
    ident_f32 = nc.inline_tensor(np.eye(128, dtype=np.float32), "ident_f32").ap()

    st = ExitStack()
    sb = lambda name, shape, dt: st.enter_context(nc.sbuf_tensor(name, shape, dt))
    psal = lambda name, shape, dt: st.enter_context(nc.psum_tensor(name, shape, dt))
    sem = lambda name: st.enter_context(nc.semaphore(name))

    t_identb = sb("identb", [128, 128], BF16)
    t_identf = sb("identf", [128, 128], F32)
    t_bqk = sb("bqk", [128, 16], F32)
    t_bqs = sb("bqs", [128, 8], F32)
    t_wrt = sb("wrt", [128, NKC * NE], F32)
    t_b1c = sb("b1cols", [128, NE * 32], F32)
    t_eids = sb("eids_sb", [NE, 1], F32)
    t_ones8 = sb("ones8_sb", [NE, 1], F32)
    t_ebase = sb("ebase_sb", [NE, 1], F32)
    t_bo = sb("bo_bc", [128, C], F32)
    t_gam = sb("gam_bc", [128, C], F32)
    t_bet = sb("bet_bc", [128, C], F32)

    eng_sems = {e: sem("e_" + e) for e in ENGINES}
    q_setup = sem("q_setup")
    q_wsetup = sem("q_wsetup")
    q_wk = sem("q_wk")
    q_wv = sem("q_wv")
    q_wq = sem("q_wq")
    q_x = [sem(f"q_x{i}") for i in range(6)]
    q_spill = sem("q_spill")
    q_idx = sem("q_idx")
    q_idxbc = sem("q_idxbc")
    q_b1c = sem("q_b1c")
    q_gsetup = sem("q_gsetup")
    q_xrel = [sem(f"q_xrel{i}") for i in range(4)]
    q_xrel2 = [sem(f"q_xrel2{i}") for i in range(4)]
    q_slot = [sem(f"q_slot{i}") for i in range(32)]
    q_out2 = [sem("q_out0"), sem("q_out1")]
    q_xg2 = [sem("q_xg0"), sem("q_xg1")]
    q_wst = [sem("q_wst0"), sem("q_wst1")]
    q_zg = sem("q_zg")
    q_dst = sem("q_dst")
    q_dst2 = sem("q_dst2")
    q_wob = sem("q_wob")
    q_xgl = [sem("q_xgl0"), sem("q_xgl1")]
    q_ygs = [sem(f"q_ygs{i}") for i in range(4)]
    q_ygr = [sem("q_ygr0"), sem("q_ygr1")]

    sched = Sched()

    a_st = ExitStack()
    asb = lambda name, shape, dt: a_st.enter_context(nc.sbuf_tensor(name, shape, dt))

    t_wab = asb("wab", [128, NKC * 3 * C], BF16)
    t_wpb = asb("wpb", [128, NKC * C], BF16)
    t_kring = asb("kring", [128, NSL * C], BF16)
    t_vring = asb("vring", [128, NSL * C], BF16)
    t_masks = asb("masks_sb", [128, 3 * W3], BF16)
    t_bvbc = asb("bv_bc", [128, C], BF16)
    t_bpbc = asb("bp_bc", [128, C], BF16)
    t_xc = asb("xc", [128, 2 * C], BF16)
    t_xTb4 = asb("xTb4", [128, NKC * 512], BF16)
    t_q4 = asb("q4", [128, NKC * 512], BF16)
    stream = asb("stream", [128, NSL * C], F32)
    t_scf = asb("scf", [128, 4 * W3], F32)
    t_prob = asb("prob", [128, 2 * 4 * W3], BF16)
    t_probT = asb("probT", [128, 2 * 4 * W3], BF16)
    t_ao = asb("attn_out", [128, C], BF16)
    t_aon = asb("attn_out_n", [128, C], BF16)
    t_aoT = asb("attn_outT", [128, C], BF16)
    t_den = asb("den", [128, NH], F32)
    t_rden = asb("rden", [128, NH], F32)
    t_w = asb("wbuf", [128, C], F32)
    t_cent = asb("centbuf", [128, C], F32)
    t_xT32 = asb("xT32", [128, C], F32)
    t_lg = asb("lgs", [128, 8], F32)
    t_mx8 = asb("mx8", [128, 8], F32)
    t_idxu = asb("idxu", [128, 8], U32)
    t_idxf = asb("idxf", [128, 1], F32)
    t_stat = asb("stat", [128, 8], F32)

    apsal = lambda name, shape, dt: a_st.enter_context(nc.psum_tensor(name, shape, dt))
    p_tra = apsal("p_tra", [128, 1024], BF16)
    p_trb = apsal("p_trb", [128, 1024], BF16)
    p_v = apsal("p_v", [128, 512], F32)
    p_sc4 = [apsal(f"p_sc4_{i}", [128, 512], F32) for i in range(4)]
    p_misc = apsal("p_misc", [128, 512], F32)

    sched.new_block()
    S = sched.op

    n_setup = [0]
    n_gsetup = [0]

    def setup_dma(engname, fn):
        if engname == "gpsimd":
            S(engname, lambda e, fn=fn: fn(e).then_inc(q_gsetup, 16))
            n_gsetup[0] += 16
        else:
            S(engname, lambda e, fn=fn: fn(e).then_inc(q_setup, 16))
            n_setup[0] += 16

    setup_dma("sync", lambda e: e.dma_start(t_identb[:], ident_bf))
    setup_dma("sync", lambda e: e.dma_start(t_identf[:], ident_f32))
    for v in range(3):
        setup_dma("gpsimd", lambda e, v=v: e.dma_start(
            t_masks[:, v * W3:(v + 1) * W3], d_msk[v]))
    setup_dma("sync", lambda e: e.dma_start(
        t_bqk[:], bass.AP(d_ba.tensor, 0, [[1, 128], [128, 16]]),
        allow_slow_non_contiguous=True))

    def bcast_src(dap, off, n):
        return bass.AP(dap.tensor, off, [[0, 128], [1, n]])

    setup_dma("gpsimd", lambda e: e.dma_start(t_bvbc[:], bcast_src(d_ba, 2048, C)))
    setup_dma("gpsimd", lambda e: e.dma_start(t_bpbc[:], bcast_src(d_bp, 0, C)))
    setup_dma("sync", lambda e: e.dma_start(t_bo[:], bcast_src(d_bo, 0, C)))
    setup_dma("sync", lambda e: e.dma_start(t_gam[:], bcast_src(d_gam, 0, C)))
    setup_dma("sync", lambda e: e.dma_start(t_bet[:], bcast_src(d_bet, 0, C)))
    for kc in range(NKC):
        setup_dma("sync", lambda e, kc=kc: e.dma_start(
            t_wrt[:, kc * NE:(kc + 1) * NE], d_wrt[kc * 128:(kc + 1) * 128, :]))
    setup_dma("sync", lambda e: e.dma_start(t_eids[:], d_eids))
    setup_dma("sync", lambda e: e.dma_start(t_ones8[:], d_ones8))
    SETUP_TOTAL = n_setup[0]



    nws = [0, 0]
    last_cast = [None, None]
    w_wk = None
    for si in range(16):
        kc, h = si // 2, si % 2
        sb2 = si % 2
        lw = [last_cast[sb2]] if last_cast[sb2] is not None else []
        srcap = d_wa[kc * 128:(kc + 1) * 128, C + h * 512: C + (h + 1) * 512]
        nws[sb2] += 16
        S("scalar", lambda e, srcap=srcap, sb2=sb2: e.dma_start(
            t_w[:, sb2 * 512:(sb2 + 1) * 512], srcap).then_inc(q_wst[sb2], 16),
          waits=lw)
        wsw = (q_wst[sb2], nws[sb2])
        dst = t_wab[:, kc * 3 * C + C + h * 512: kc * 3 * C + C + (h + 1) * 512]
        last_cast[sb2] = S("vector", lambda e, dst=dst, sb2=sb2: e.tensor_copy(
            dst, t_w[:, sb2 * 512:(sb2 + 1) * 512]), waits=[wsw])
        w_wk = last_cast[sb2]
    for qsem, off in ((q_wq, 0), (q_wv, 2 * C)):
        for kc in range(NKC):
            S("gpsimd", lambda e, kc=kc, off=off, qsem=qsem: e.dma_start(
                t_wab[:, kc * 3 * C + off: kc * 3 * C + off + C],
                d_wa[kc * 128:(kc + 1) * 128, off:off + C]).then_inc(qsem, 16))
    w_wkv = (q_wv, 8 * 16)
    w_wkvq = (q_wq, 8 * 16)
    for kc in range(NKC):
        S("gpsimd", lambda e, kc=kc: e.dma_start(
            t_wpb[:, kc * C:(kc + 1) * C],
            d_wp[kc * 128:(kc + 1) * 128, :]).then_inc(q_wsetup, 16))
    w_wsetup = (q_wsetup, 8 * 16)
    n_b1c = [0]
    for j in range(32):
        S("scalar", lambda e, j=j: e.dma_start(
            t_b1c[:, j * 8:(j + 1) * 8],
            bass.AP(d_b1.tensor, j * 128, [[1, 128], [FF, 8]]),
            allow_slow_non_contiguous=True).then_inc(q_b1c, 16))
        n_b1c[0] += 16
    w_b1c = (q_b1c, n_b1c[0])
    S("gpsimd", lambda e: e.dma_start(d_wob, d_wo).then_inc(q_wob, 16))
    w_setup = (q_setup, SETUP_TOTAL)
    w_gsetup = (q_gsetup, n_gsetup[0])

    tok_bqs = S("vector", lambda e: e.tensor_scalar(
        t_bqs[:], t_bqk[:, 0:8], 0.125, None, op0=ALU.mult), waits=[w_setup])

    trb_ev = [None, None]

    tok_cast = {}
    tok_trE = {}
    tok_kev = {}
    tok_vev = {}
    tok_qev = {}
    tok_kmm = {}
    tok_qmm = {}
    tok_v_mm = {}
    tok_scg = {}
    tok_scg_h = {}
    tok_mask = {}
    tok_mask_h = {}
    tok_exp = {}
    tok_red = {}
    tok_recip = {}
    tok_ptpe = {}
    tok_ptev = {}
    tok_av = {}
    tok_aoev = {}
    tok_normao = {}
    tok_aoT_pe = {}
    tok_aoT_ev = {}
    tok_y1_mm = {}
    tok_proj_ev = {}
    tok_logit_ev = {}
    tok_xnew = {}
    tok_trP = {}

    nx_load = [0] * 6
    n_spill = [0]
    avq = {}
    chunk_tiles = [list(range(4 * c2, min(4 * c2 + 4, NTQ))) for c2 in range(5)]

    for ch, tiles in enumerate(chunk_tiles):
        Wc = len(tiles) * 128
        qoff = 256 if ch == 0 else 0

        for ti, tq in enumerate(tiles):
            sl = tq % NSL
            qi = tq % 6
            nx_load[qi] += 16
            xw = (q_x[qi], nx_load[qi])
            dst = stream[:, sl * C:(sl + 1) * C]
            lw = []
            if tq - NSL >= 2:
                tt_old = tq - NSL - 2
                lw.append((q_spill, 16 * n_spill[0]))
                lw.append(tok_logit_ev[tt_old])
            elif tq - NSL >= 0:
                lw.append(tok_cast[tq - NSL])
            S("sync", lambda e, dst=dst, tq=tq, qi=qi: e.dma_start(
                dst, d_x[tq * 128:(tq + 1) * 128, :]).then_inc(q_x[qi], 16), waits=lw)

            cbuf = t_xc[:, (tq % 2) * C:(tq % 2 + 1) * C]
            cw = [xw]
            if tq - 2 >= 0:
                cw.append(tok_trP[tq - 2])
            tok_cast[tq] = S("scalar", lambda e, cbuf=cbuf, dst=dst: e.copy(cbuf, dst),
                             waits=cw)

            def mk_tr(eng, cbuf=cbuf):
                last = None
                for j in range(NKC):
                    last = eng.transpose(p_tra[:, j * 128:(j + 1) * 128],
                                         cbuf[:, j * 128:(j + 1) * 128], t_identb[:])
                return last
            trw = [tok_cast[tq], w_setup]
            if tq >= 1:
                trw.append(tok_trE[tq - 1])
            tok_trP[tq] = S("tensor", mk_tr, waits=trw)

            xdst = (t_xTb4[:].rearrange("p (kc t) -> p kc t", kc=NKC)
                    [:, :, ti * 128:(ti + 1) * 128])
            evw = [tok_trP[tq]]
            if ch >= 1 and ti == 0:
                evw.append(tok_qmm[(ch - 1, NKC - 1)])
            tok_trE[tq] = S("vector", lambda e, xdst=xdst: e.tensor_copy(
                xdst, p_tra[:, 0:C].rearrange("p (kc t) -> p kc t", kc=NKC)),
                waits=evw)

        for fc in range(NKC):
            def mk_k(eng, fc=fc, Wc=Wc):
                last = None
                for kc in range(NKC):
                    last = eng.matmul(
                        p_sc4[fc % 4][:, 0:Wc],
                        t_wab[:, kc * 3 * C + C + fc * 128: kc * 3 * C + C + (fc + 1) * 128],
                        t_xTb4[:, kc * 512:kc * 512 + Wc],
                        start=(kc == 0), stop=(kc == NKC - 1))
                return last
            kw = [w_wk]
            if fc == 0:
                kw += [tok_trE[t2] for t2 in tiles]
                if ch >= 1:
                    lastqb_prev = 3 if ch == 1 else 4 * (ch - 1) + 3
                    kw.append(tok_mask[(lastqb_prev - 2, 3)])
            elif fc == 4:
                kw.append(tok_kev[(ch, 3)])
            tok_kmm[(ch, fc)] = S("tensor", mk_k, waits=kw)
            if fc in (3, 7):
                for fc2 in range(fc - 3, fc + 1):
                    s0 = tiles[0] % NSL
                    kdst = (t_kring[:].rearrange("p (s c) -> p s c", s=NSL)
                            [:, s0:s0 + len(tiles), fc2 * 128:(fc2 + 1) * 128])
                    ksrc = (p_sc4[fc2 % 4][:, 0:Wc]
                            .rearrange("p (t c) -> p t c", c=128))
                    tok_kev[(ch, fc2)] = S("scalar", lambda e, kdst=kdst, ksrc=ksrc,
                                           fc2=fc2:
                                           e.activation(kdst, ksrc, AF.Identity,
                                                        bias=t_bqk[:, 8 + fc2:9 + fc2]),
                                           waits=[tok_kmm[(ch, fc)], w_setup])

        for ti, tq in enumerate(tiles):
            for vh in range(2):
                def mk_v(eng, vh=vh, ti=ti):
                    last = None
                    for kc in range(NKC):
                        last = eng.matmul(
                            p_v[:],
                            t_xTb4[:, kc * 512 + ti * 128: kc * 512 + (ti + 1) * 128],
                            t_wab[:, kc * 3 * C + 2 * C + vh * 512:
                                  kc * 3 * C + 2 * C + (vh + 1) * 512],
                            start=(kc == 0), stop=(kc == NKC - 1))
                    return last
                vw = [tok_trE[tq], w_wkv]
                if vh == 1:
                    vw.append(tok_vev[(tq, 0)])
                elif (tq - 1, 1) in tok_vev:
                    vw.append(tok_vev[(tq - 1, 1)])
                tok_v_mm[(tq, vh)] = S("tensor", mk_v, waits=vw)
                vdst = t_vring[:, (tq % NSL) * C + vh * 512:
                               (tq % NSL) * C + (vh + 1) * 512]
                ew = [tok_v_mm[(tq, vh)], w_setup, w_gsetup]
                if tq - NSL in avq:
                    ew.append(avq[tq - NSL])
                tok_vev[(tq, vh)] = S("vector", lambda e, vdst=vdst, vh=vh:
                                      e.scalar_tensor_tensor(
                                          vdst, p_v[:], 1.0,
                                          t_bvbc[:, vh * 512:(vh + 1) * 512],
                                          op0=ALU.mult, op1=ALU.add), waits=ew)

        for fc in range(NKC):
            def mk_q(eng, fc=fc, Wc=Wc, qoff=qoff):
                last = None
                for kc in range(NKC):
                    last = eng.matmul(
                        p_sc4[fc % 4][:, qoff:Wc],
                        t_wab[:, kc * 3 * C + fc * 128: kc * 3 * C + (fc + 1) * 128],
                        t_xTb4[:, kc * 512 + qoff:kc * 512 + Wc],
                        start=(kc == 0), stop=(kc == NKC - 1))
                return last
            qw = [tok_bqs, w_wkvq]
            if fc == 0:
                qw.append(tok_kev[(ch, NKC - 1)])
            elif fc == 4:
                qw.append(tok_qev[(ch, 3)])
            tok_qmm[(ch, fc)] = S("tensor", mk_q, waits=qw)
            if fc in (3, 7):
                for fc2 in range(fc - 3, fc + 1):
                    qdst = t_q4[:, fc2 * 512 + qoff: fc2 * 512 + Wc]
                    tok_qev[(ch, fc2)] = S("scalar", lambda e, qdst=qdst, fc2=fc2,
                                           Wc=Wc, qoff=qoff:
                                           e.activation(qdst,
                                                        p_sc4[fc2 % 4][:, qoff:Wc],
                                                        AF.Identity, scale=0.125,
                                                        bias=t_bqs[:, fc2:fc2 + 1]),
                                           waits=[tok_qmm[(ch, fc)]])

        qbs = [2, 3] if ch == 0 else list(tiles)
        for tq in qbs:
            tt = tq - 2
            ti = tq - 4 * ch
            mvar = tt if tt < 2 else 2
            for s in range(5):
                if s < 4:
                    G = s
                    g2 = G % 2
                    for hloc in range(4):
                        def mk_sc(eng, G=G, tq=tq, ti=ti, hloc=hloc):
                            last = None
                            h = G * 4 + hloc
                            fc, r0 = h // 2, (h % 2) * 64
                            for kr in range(3):
                                ks = (tq - 2 + kr) % NSL
                                last = eng.matmul(
                                    p_sc4[hloc][:, kr * 128:(kr + 1) * 128],
                                    t_q4[r0:r0 + 64, fc * 512 + ti * 128:
                                         fc * 512 + (ti + 1) * 128],
                                    t_kring[r0:r0 + 64,
                                            ks * C + fc * 128: ks * C + (fc + 1) * 128],
                                    start=True, stop=True)
                            return last
                        sw = []
                        if G == 0 and tq == (2 if ch == 0 else tiles[0]):
                            sw.append(tok_qev[(ch, NKC - 1)])
                        pmk = (tt, G - 1, hloc) if G >= 1 else (tt - 1, 3, hloc)
                        if pmk in tok_mask_h:
                            sw.append(tok_mask_h[pmk])
                        tok_scg_h[(tt, G, hloc)] = S("tensor", mk_sc, waits=sw)
                    tok_scg[(tt, G)] = tok_scg_h[(tt, G, 3)]

                    for hloc in range(4):
                        mwh = [tok_scg_h[(tt, G, hloc)]]
                        if hloc == 0:
                            mwh += [w_setup, w_gsetup]
                            if G >= 1:
                                mwh.append(tok_exp[(tt, G - 1)])
                            elif tt >= 1:
                                mwh.append(tok_exp[(tt - 1, 3)])
                        tok_mask_h[(tt, G, hloc)] = S(
                            "vector", lambda e, hloc=hloc, mvar=mvar:
                            e.tensor_tensor(
                                t_scf[:, hloc * W3:(hloc + 1) * W3],
                                p_sc4[hloc][:, 0:W3],
                                t_masks[:, mvar * W3:(mvar + 1) * W3], op=ALU.add),
                            waits=mwh)
                    tok_mask[(tt, G)] = tok_mask_h[(tt, G, 3)]

                    pbuf = t_prob[:, g2 * 4 * W3:(g2 + 1) * 4 * W3]
                    pk = (tt, G - 2) if G >= 2 else (tt - 1, G + 2)
                    ew = [tok_mask[(tt, G)]]
                    if pk in tok_red:
                        ew += [tok_red[pk], tok_ptpe[pk + (1,)]]
                    tok_exp[(tt, G)] = S("scalar", lambda e, pbuf=pbuf: e.activation(
                        pbuf, t_scf[:], AF.Exp), waits=ew)

                if s >= 1:
                    G = s - 1
                    g2 = G % 2
                    pbuf = t_prob[:, g2 * 4 * W3:(g2 + 1) * 4 * W3]
                    for sub in range(2):
                        def mk_pt(eng, pbuf=pbuf, sub=sub):
                            last = None
                            for hl in range(2):
                                for kr in range(3):
                                    last = eng.transpose(
                                        p_trb[:, (hl * 3 + kr) * 128:
                                              (hl * 3 + kr + 1) * 128],
                                        pbuf[:, (2 * sub + hl) * W3 + kr * 128:
                                             (2 * sub + hl) * W3 + (kr + 1) * 128],
                                        t_identb[:])
                            return last
                        tok_ptpe[(tt, G, sub)] = S("tensor", mk_pt,
                                                   waits=[tok_exp[(tt, G)], trb_ev[1]])
                        ptdst = t_probT[:, g2 * 4 * W3 + sub * 768:
                                        g2 * 4 * W3 + (sub + 1) * 768]
                        ew = [tok_ptpe[(tt, G, sub)]]
                        ak = ((tt, G - 2, 2 * sub + 1) if G >= 2
                              else (tt - 1, G + 2, 2 * sub + 1))
                        if ak in tok_av:
                            ew.append(tok_av[ak])
                        tok_ptev[(tt, G, sub)] = S("vector",
                                                   lambda e, ptdst=ptdst: e.tensor_copy(
                                                       ptdst, p_trb[:, 0:768]), waits=ew)
                        trb_ev[1] = tok_ptev[(tt, G, sub)]

                    for hloc in range(4):
                        h = G * 4 + hloc
                        sub = hloc // 2

                        def mk_av(eng, g2=g2, hloc=hloc, h=h, G=G, tq=tq):
                            last = None
                            for kr in range(3):
                                ks = (tq - 2 + kr) % NSL
                                last = eng.matmul(
                                    p_misc[:, (G % 2) * 256 + hloc * 64:
                                           (G % 2) * 256 + (hloc + 1) * 64],
                                    t_probT[:, g2 * 4 * W3 + hloc * W3 + kr * 128:
                                            g2 * 4 * W3 + hloc * W3 + (kr + 1) * 128],
                                    t_vring[:, ks * C + h * 64: ks * C + (h + 1) * 64],
                                    start=(kr == 0), stop=(kr == 2))
                            return last
                        aw = [tok_ptev[(tt, G, sub)], tok_vev[(tq, 1)]]
                        if G == 0 and hloc == 0 and tt >= 1:
                            aw.append(tok_logit_ev[tt - 1])
                        pk2 = (tt, G - 1) if G >= 1 else (tt - 1, 3)
                        if hloc == 0 and pk2 in tok_aoev:
                            aw.append(tok_aoev[pk2])
                        tok_av[(tt, G, hloc)] = S("tensor", mk_av, waits=aw)

                    tok_aoev[(tt, G)] = S("vector", lambda e, G=G: e.tensor_copy(
                        t_ao[:, G * 256:(G + 1) * 256],
                        p_misc[:, (G % 2) * 256:(G % 2 + 1) * 256]),
                        waits=[tok_av[(tt, G, 3)]])

                    tok_red[(tt, G)] = S("vector", lambda e, G=G, pbuf=pbuf:
                        e.reduce_sum(
                            t_den[:, G * 4:(G + 1) * 4].rearrange("p h -> p h ()"),
                            pbuf.rearrange("p (h j) -> p h j", h=4), axis=AX.X),
                        waits=[tok_exp[(tt, G)]])

            avq[tt] = tok_av[(tt, 3, 3)]

            tok_recip[tt] = S("vector", lambda e: e.reciprocal(t_rden[:], t_den[:]),
                              waits=[tok_red[(tt, 3)]])
            nw = [tok_recip[tt], tok_aoev[(tt, 3)]]
            if tt >= 1:
                nw.append(tok_aoT_pe[tt - 1])
            tok_normao[tt] = S("vector", lambda e: e.tensor_tensor(
                t_aon[:].rearrange("p (h d) -> p h d", h=16),
                t_ao[:].rearrange("p (h d) -> p h d", h=16),
                t_rden[:].rearrange("p h -> p h ()").broadcast_to([128, NH, HD]),
                op=ALU.mult), waits=nw)

            def mk_aoT(eng):
                last = None
                for j in range(NKC):
                    last = eng.transpose(p_trb[:, j * 128:(j + 1) * 128],
                                         t_aon[:, j * 128:(j + 1) * 128], t_identb[:])
                return last
            aw = [tok_normao[tt], trb_ev[1]]
            tok_aoT_pe[tt] = S("tensor", mk_aoT, waits=aw)
            ew = [tok_aoT_pe[tt]]
            if tt >= 1:
                ew.append(tok_y1_mm[(tt - 1, 1)])
            tok_aoT_ev[tt] = S("vector", lambda e: e.tensor_copy(t_aoT[:], p_trb[:, 0:C]),
                               waits=ew)
            trb_ev[1] = tok_aoT_ev[tt]

            for nh2 in range(2):
                def mk_pj(eng, nh2=nh2):
                    last = None
                    for kc in range(NKC):
                        last = eng.matmul(
                            p_misc[:],
                            t_aoT[:, kc * 128:(kc + 1) * 128],
                            t_wpb[:, kc * C + nh2 * 512: kc * C + (nh2 + 1) * 512],
                            start=(kc == 0), stop=(kc == NKC - 1))
                    return last
                pw = [tok_aoT_ev[tt], w_wsetup]
                if nh2 == 1:
                    pw.append(tok_proj_ev[(tt, 0)])
                tok_y1_mm[(tt, nh2)] = S("tensor", mk_pj, waits=pw)
                ew = [tok_y1_mm[(tt, nh2)], w_setup, w_gsetup]
                tok_proj_ev[(tt, nh2)] = S("vector", lambda e, nh2=nh2:
                                           e.scalar_tensor_tensor(
                                               t_w[:, nh2 * 512:(nh2 + 1) * 512], p_misc[:], 1.0,
                                               t_bpbc[:, nh2 * 512:(nh2 + 1) * 512],
                                               op0=ALU.mult, op1=ALU.add), waits=ew)

            tok_res = S("vector", lambda e, tq=tq: e.tensor_tensor(
                t_w[:], t_w[:], stream[:, (tq % NSL) * C:(tq % NSL + 1) * C], op=ALU.add),
                waits=[tok_proj_ev[(tt, 0)], tok_proj_ev[(tt, 1)]])

            tok_ns = S("vector", lambda e: e.reduce_sum(t_stat[:, 0:1], t_w[:], axis=AX.X,
                                                        negate=True), waits=[tok_res])
            tok_nm = S("vector", lambda e: e.tensor_scalar(t_stat[:, 1:2], t_stat[:, 0:1],
                                                           1.0 / C, None, op0=ALU.mult),
                       waits=[tok_ns])
            tok_ct = S("vector", lambda e: e.tensor_scalar(t_cent[:], t_w[:], t_stat[:, 1:2],
                                                           None, op0=ALU.add), waits=[tok_nm])
            tok_sq = S("vector", lambda e: e.tensor_tensor(t_w[:], t_cent[:], t_cent[:],
                                                           op=ALU.mult), waits=[tok_ct])
            tok_vr = S("vector", lambda e: e.reduce_sum(t_stat[:, 2:3], t_w[:], axis=AX.X),
                       waits=[tok_sq])
            tok_v2 = S("vector", lambda e: e.tensor_scalar(t_stat[:, 3:4], t_stat[:, 2:3],
                                                           1.0 / C, EPS1, op0=ALU.mult,
                                                           op1=ALU.add), waits=[tok_vr])
            tok_ln = S("scalar", lambda e: e.activation(t_stat[:, 4:5], t_stat[:, 3:4], AF.Ln),
                       waits=[tok_v2])
            tok_rs = S("scalar", lambda e: e.activation(t_stat[:, 5:6], t_stat[:, 4:5], AF.Exp,
                                                        scale=-0.5), waits=[tok_ln])
            tok_xnew[tt] = S("vector", lambda e, tq=tq: e.tensor_scalar(
                stream[:, (tq % NSL) * C:(tq % NSL + 1) * C], t_cent[:], t_stat[:, 5:6],
                None, op0=ALU.mult), waits=[tok_rs])

            n_spill[0] += 1
            S("sync", lambda e, tt=tt, tq=tq: e.dma_start(
                d_x1[tt * 128:(tt + 1) * 128, :],
                stream[:, (tq % NSL) * C:(tq % NSL + 1) * C]).then_inc(q_spill, 16),
              waits=[tok_xnew[tt]])

            tok_rtr_ev = {}
            for bt in range(2):
                def mk_rtr(eng, bt=bt, tq=tq):
                    last = None
                    for j in range(4):
                        last = eng.transpose(
                            p_misc[:, j * 128:(j + 1) * 128],
                            stream[:, (tq % NSL) * C + (bt * 4 + j) * 128: (tq % NSL) * C + (bt * 4 + j + 1) * 128],
                            t_identf[:])
                    return last
                rw = [tok_xnew[tt]]
                rw.append(tok_proj_ev[(tt, 1)] if bt == 0 else tok_rtr_ev[0])
                tok_rtr_pe = S("tensor", mk_rtr, waits=rw)
                tok_rtr_ev[bt] = S("vector", lambda e, bt=bt: e.tensor_copy(
                    t_xT32[:, bt * 512:(bt + 1) * 512], p_misc[:]), waits=[tok_rtr_pe])

            def mk_rmm(eng):
                last = None
                for kc in range(NKC):
                    last = eng.matmul(p_misc[:, 0:NE],
                                      t_xT32[:, kc * 128:(kc + 1) * 128],
                                      t_wrt[:, kc * NE:(kc + 1) * NE],
                                      start=(kc == 0), stop=(kc == NKC - 1))
                return last
            tok_rmm = S("tensor", mk_rmm, waits=[tok_rtr_ev[0], tok_rtr_ev[1], w_setup])
            tok_logit_ev[tt] = S("vector", lambda e: e.tensor_copy(t_lg[:], p_misc[:, 0:NE]),
                                 waits=[tok_rmm])
            tok_m8 = S("vector", lambda e: e.max(t_mx8[:], t_lg[:]), waits=[tok_logit_ev[tt]])
            tok_mi = S("vector", lambda e: e.max_index(t_idxu[:], t_mx8[:], t_lg[:]),
                       waits=[tok_m8])
            tok_if = S("vector", lambda e: e.tensor_scalar(t_idxf[:], t_idxu[:, 0:1], 0.0,
                                                           None, op0=ALU.add),
                       waits=[tok_mi] + ([(q_idx, 16 * tt)] if tt >= 1 else []))
            S("sync", lambda e, tt=tt: e.dma_start(
                d_idx.rearrange("(a p) -> a p", p=128)[tt].rearrange("(p o) -> p o", o=1),
                t_idxf[:]).then_inc(q_idx, 16), waits=[tok_if])

    S("sync", lambda e: e.nop(), waits=[(q_spill, 16 * NTT), (q_idx, 16 * NTT)])

    a_st.close()
    b_st = ExitStack()
    bsb = lambda name, shape, dt: b_st.enter_context(nc.sbuf_tensor(name, shape, dt))

    t_ring = bsb("ring", [128, 32 * 1024], BF16)
    t_idxbc = bsb("idx_bc", [128, 2048], F32)
    t_rank = bsb("rank", [128, 2048], F32)
    t_dsf = bsb("dsf", [128, 16], F32)
    t_dsi = bsb("dsi", [128, 16], I32)
    t_xrb = bsb("xrb", [128, 2 * C], BF16)
    t_xgl = bsb("xgl", [128, 2 * C], BF16)
    t_xgT = bsb("xgT", [128, 2 * NKC * 512], BF16)
    t_zg = bsb("zg", [128, 32 * 512], BF16)
    t_yst = bsb("yst", [128, 4 * C], F32)
    t_yg = bsb("yg", [128, 2 * C], F32)
    t_xrel2 = bsb("xrel2", [128, 2 * C], F32)
    t_y2 = bsb("y2", [128, 2 * C], F32)
    t_ct2 = bsb("ct2", [128, 2 * C], F32)
    t_st2 = bsb("stat2", [128, 16], F32)

    bpsal = lambda name, shape, dt: b_st.enter_context(nc.psum_tensor(name, shape, dt))
    p_i = [bpsal(f"p_i{j}", [128, 512], F32) for j in range(4)]
    p_y2 = [bpsal(f"p_y2{j}", [128, 512], F32) for j in range(2)]
    p_trB = bpsal("p_trB", [128, 1024], BF16)

    sched.new_block()
    S = sched.op

    nw_slot = [0] * 32
    slot_last_reader = [None] * 32
    load_ctr = [0]
    n_xrel = [0] * 4
    n_xrel2 = [0] * 4
    n_out2 = [0, 0]

    tok_idxbc = S("sync", lambda e: e.dma_start(
        t_idxbc[:], bass.AP(d_idx.tensor, 0, [[0, 128], [1, 2048]])).then_inc(
        q_idxbc, 16))
    w_idxbc = (q_idxbc, 16)

    tok_ebase = S("vector", lambda e: e.tensor_scalar(
        t_ebase[:], t_eids[:], 512.0, -1.0, op0=ALU.mult, op1=ALU.add),
        waits=[w_setup])
    tok_eq8 = S("vector", lambda e: e.tensor_scalar(
        t_rank[0:8, :], t_idxbc[0:8, :], t_eids[:], None, op0=ALU.is_equal),
        waits=[w_idxbc, w_setup])
    cur, oth = 0, 32
    tok_cs = tok_eq8
    for k in [1 << i for i in range(11)]:
        t1 = S("vector", lambda e, cur=cur, oth=oth, k=k: e.tensor_copy(
            t_rank[oth:oth + 8, 0:k], t_rank[cur:cur + 8, 0:k]), waits=[tok_cs])
        tok_cs = S("vector", lambda e, cur=cur, oth=oth, k=k: e.tensor_tensor(
            t_rank[oth:oth + 8, k:2048], t_rank[cur:cur + 8, k:2048],
            t_rank[cur:cur + 8, 0:2048 - k], op=ALU.add), waits=[t1])
        cur, oth = oth, cur
    assert cur == 32
    tok_eq8b = S("vector", lambda e: e.tensor_scalar(
        t_idxbc[0:8, :], t_idxbc[0:8, :], t_eids[:], None,
        op0=ALU.is_equal), waits=[tok_cs])
    tok_tmp = S("vector", lambda e, cur=cur: e.tensor_scalar(
        t_rank[0:8, :], t_rank[cur:cur + 8, :], t_ebase[:], None, op0=ALU.add),
        waits=[tok_cs, tok_ebase])
    tok_w8 = S("vector", lambda e: e.tensor_tensor(
        t_rank[0:8, :], t_rank[0:8, :], t_idxbc[0:8, :],
        op=ALU.mult), waits=[tok_tmp, tok_eq8b])
    w8base = 0
    tok_dsum_ev = {}
    for c4 in range(4):
        bank = p_y2[c4 % 2]
        dw = [tok_w8, w_setup]
        if c4 >= 2:
            dw.append(tok_dsum_ev[c4 - 2])
        tok_dmm = S("tensor", lambda e, bank=bank, c4=c4: e.matmul(
            bank[0:1, 0:512], t_ones8[:],
            t_rank[w8base:w8base + 8, c4 * 512:(c4 + 1) * 512],
            start=True, stop=True), waits=dw)
        tok_dsum_ev[c4] = S("vector", lambda e, bank=bank, c4=c4: e.tensor_copy(
            t_rank[64:65, c4 * 512:(c4 + 1) * 512], bank[0:1, 0:512]),
            waits=[tok_dmm])
    S("sync", lambda e: e.dma_start(d_dst, t_rank[64:65, :]).then_inc(q_dst, 16),
      waits=[tok_dsum_ev[2], tok_dsum_ev[3]])
    S("sync", lambda e: e.dma_start(
        t_dsf[:], bass.AP(d_dst.tensor, 0, [[1, 128], [128, 16]]),
        allow_slow_non_contiguous=True).then_inc(q_dst2, 16),
      waits=[(q_dst, 16)])
    tok_dsi = S("vector", lambda e: e.tensor_copy(t_dsi[:], t_dsf[:]),
                waits=[(q_dst2, 16)])

    tok_z0 = S("vector", lambda e: e.memset(t_xgl[:, 0:1024], 0.0))
    for j in range(32):
        S("sync", lambda e, j=j: e.dma_start(
            d_xg[j * 128:(j + 1) * 128, :], t_xgl[:, 0:1024]).then_inc(q_zg, 16),
          waits=[tok_z0])
    w_zg = (q_zg, 32 * 16)

    pre_slot = {}
    for iq in range(4):
        for kc in range(NKC):
            L = load_ctr[0]
            s = L % 32
            nw_slot[s] += 1
            pre_slot[(iq, kc)] = (s, nw_slot[s])
            S("gpsimd", lambda e, s=s, kc=kc, iq=iq: e.dma_start(
                t_ring[:, s * 1024:(s + 1) * 1024],
                d_w1[0, kc * 128:(kc + 1) * 128,
                     iq * 1024:(iq + 1) * 1024]).then_inc(q_slot[s], 16))
            load_ctr[0] += 1

    nscat = [0, 0]
    tok_scat = {}
    for t2 in range(NTT):
        qi = t2 % 4
        n_xrel[qi] += 1
        lw = [(q_spill, 16 * NTT)]
        if t2 - 2 >= 0:
            lw.append((q_xg2[t2 % 2], nscat[t2 % 2]))
        S("gpsimd", lambda e, t2=t2, qi=qi: e.dma_start(
            t_xrb[:, (t2 % 2) * C:(t2 % 2 + 1) * C],
            d_x1[t2 * 128:(t2 + 1) * 128, :]).then_inc(q_xrel[qi], 16), waits=lw)
        xrw = (q_xrel[qi], 16 * n_xrel[qi])
        nscat[t2 % 2] += 16
        tok_scat[t2] = S("gpsimd", lambda e, t2=t2: e.indirect_dma_start(
            d_xg, IndirectOffsetOnAxis(ap=t_dsi[:, t2:t2 + 1], axis=0),
            t_xrb[:, (t2 % 2) * C:(t2 % 2 + 1) * C], None,
            bounds_check=NE * 512 - 1, oob_is_err=False).then_inc(
            q_xg2[t2 % 2], 16), waits=[xrw, tok_dsi, w_zg])
    w_scat = [(q_xg2[0], nscat[0]), (q_xg2[1], nscat[1])]

    nxgl = [0, 0]
    tok_xgtr = {}
    tok_xgev = {}
    tok_hmm = {}
    tok_gelu = {}
    tok_womm = {}
    tok_yev = {}
    nyg = [0] * 4
    tok_hmm_last = {}
    tok_womm_last = {}
    tok_gelu_last = {}

    for eix in range(NE):
        for j in range(4):
            gb2 = (eix * 4 + j) % 2
            lw = list(w_scat) if (eix == 0 and j == 0) else []
            prev = (eix, j - 2) if j >= 2 else (eix - 1, j + 2)
            if prev in tok_xgtr:
                lw.append(tok_xgtr[prev])
            nxgl[gb2] += 16
            S("sync", lambda e, eix=eix, j=j, gb2=gb2: e.dma_start(
                t_xgl[:, gb2 * C:(gb2 + 1) * C],
                d_xg[eix * 512 + j * 128: eix * 512 + (j + 1) * 128, :]).then_inc(
                q_xgl[gb2], 16), waits=lw)
            xglw = (q_xgl[gb2], nxgl[gb2])

            def mk_xt(eng, gb2=gb2):
                last = None
                for kc in range(NKC):
                    last = eng.transpose(
                        p_trB[:, kc * 128:(kc + 1) * 128],
                        t_xgl[:, gb2 * C + kc * 128: gb2 * C + (kc + 1) * 128],
                        t_identb[:])
                return last
            tw = [xglw, w_setup]
            pj = (eix, j - 1) if j >= 1 else (eix - 1, 3)
            if pj in tok_xgev:
                tw.append(tok_xgev[pj])
            tok_xgtr[(eix, j)] = S("tensor", mk_xt, waits=tw)
            xdst = (t_xgT[:, (eix % 2) * NKC * 512:(eix % 2 + 1) * NKC * 512]
                    .rearrange("p (kc t) -> p kc t", kc=NKC)
                    [:, :, j * 128:(j + 1) * 128])
            ew = [tok_xgtr[(eix, j)]]
            if (eix - 2) in tok_hmm_last and j == 0:
                ew.append(tok_hmm_last[eix - 2])
            tok_xgev[(eix, j)] = S("vector", lambda e, xdst=xdst: e.tensor_copy(
                xdst, p_trB[:, 0:C].rearrange("p (kc t) -> p kc t", kc=NKC)),
                waits=ew)

        for iq in range(4):
            if eix == 0:
                slot_of_kc = {kc: pre_slot[(iq, kc)] for kc in range(NKC)}
            else:
                slot_of_kc = {}
                for kc in range(NKC):
                    L = load_ctr[0]
                    s = L % 32
                    nw_slot[s] += 1
                    slot_of_kc[kc] = (s, nw_slot[s])
                    lw = ([slot_last_reader[s]]
                          if slot_last_reader[s] is not None else [])
                    S("gpsimd", lambda e, s=s, eix=eix, kc=kc, iq=iq: e.dma_start(
                        t_ring[:, s * 1024:(s + 1) * 1024],
                        d_w1[eix, kc * 128:(kc + 1) * 128,
                             iq * 1024:(iq + 1) * 1024]).then_inc(q_slot[s], 16),
                      waits=lw)
                    load_ctr[0] += 1
            for icl in range(8):
                icg = iq * 8 + icl
                bank = p_i[icg % 4]

                def mk_im(eng, bank=bank, icl=icl, eix=eix,
                          slot_of_kc=dict(slot_of_kc)):
                    last = None
                    for kc in range(NKC):
                        s2, _ = slot_of_kc[kc]
                        last = eng.matmul(
                            bank[:],
                            t_ring[:, s2 * 1024 + icl * 128: s2 * 1024 + (icl + 1) * 128],
                            t_xgT[:, (eix % 2) * NKC * 512 + kc * 512:
                                  (eix % 2) * NKC * 512 + (kc + 1) * 512],
                            start=(kc == 0), stop=(kc == NKC - 1))
                    return last
                mw = [tok_xgev[(eix, 3)]]
                mw += [(q_slot[s2], 16 * n2) for (s2, n2) in slot_of_kc.values()]
                if (eix, icg - 4) in tok_gelu:
                    mw.append(tok_gelu[(eix, icg - 4)])
                elif eix >= 1 and icg < 4:
                    mw.append(tok_gelu[(eix - 1, 28 + icg)])
                tok_hmm[(eix, icg)] = S("tensor", mk_im, waits=mw)

                gw = [tok_hmm[(eix, icg)], w_b1c]
                if icg == 0 and (eix - 1) in tok_womm_last:
                    gw.append(tok_womm_last[eix - 1])
                tok_gelu[(eix, icg)] = S("scalar", lambda e, bank=bank, eix=eix,
                                         icg=icg:
                                         e.activation(
                                             t_zg[:, icg * 512:(icg + 1) * 512],
                                             bank[:], AF.Gelu,
                                             bias=t_b1c[:, icg * 8 + eix:
                                                        icg * 8 + eix + 1]),
                                         waits=gw)
            for kc in range(NKC):
                s2, _ = slot_of_kc[kc]
                slot_last_reader[s2] = tok_hmm[(eix, iq * 8 + 7)]
        tok_hmm_last[eix] = tok_hmm[(eix, 31)]
        tok_gelu_last[eix] = tok_gelu[(eix, 31)]

        wo_slot = {}
        for ic in range(32):
            L = load_ctr[0]
            s = L % 32
            nw_slot[s] += 1
            wo_slot[ic] = (s, nw_slot[s])
            lw = [slot_last_reader[s]] if slot_last_reader[s] is not None else []
            if ic == 0:
                lw.append((q_wob, 16))
            S("gpsimd", lambda e, s=s, ic=ic: e.dma_start(
                t_ring[:, s * 1024:(s + 1) * 1024],
                d_wob[ic * 128:(ic + 1) * 128, :]).then_inc(q_slot[s], 16),
              waits=lw)
            load_ctr[0] += 1

        for j in range(4):
            for ch2 in range(2):
                def mk_wo(eng, j=j, ch2=ch2, wo_slot=dict(wo_slot)):
                    last = None
                    for ic in range(32):
                        s2, _ = wo_slot[ic]
                        last = eng.matmul(
                            p_y2[ch2][:],
                            t_zg[:, ic * 512 + j * 128: ic * 512 + (j + 1) * 128],
                            t_ring[:, s2 * 1024 + ch2 * 512: s2 * 1024 + (ch2 + 1) * 512],
                            start=(ic == 0), stop=(ic == 31))
                    return last
                ww = [tok_gelu_last[eix]]
                ww += [(q_slot[s2], 16 * n2) for (s2, n2) in wo_slot.values()]
                if j >= 1:
                    ww.append(tok_yev[(eix, j - 1, ch2)])
                elif eix >= 1:
                    ww.append(tok_yev[(eix - 1, 3, ch2)])
                tok_womm[(eix, j, ch2)] = S("tensor", mk_wo, waits=ww)
                ew = [tok_womm[(eix, j, ch2)]]
                if eix >= 1:
                    ew.append((q_ygs[j], nyg[j]))
                tok_yev[(eix, j, ch2)] = S("vector", lambda e, j=j, ch2=ch2:
                                           e.tensor_copy(
                                               t_yst[:, j * C + ch2 * 512:
                                                     j * C + ch2 * 512 + 512],
                                               p_y2[ch2][:]), waits=ew)
        for ic in range(32):
            s2, _ = wo_slot[ic]
            slot_last_reader[s2] = tok_womm[(eix, 3, 1)]
        tok_womm_last[eix] = tok_womm[(eix, 3, 1)]
        for j in range(4):
            nyg[j] += 16
            S("sync", lambda e, eix=eix, j=j: e.dma_start(
                d_yg[eix * 512 + j * 128: eix * 512 + (j + 1) * 128, :],
                t_yst[:, j * C:(j + 1) * C]).then_inc(q_ygs[j], 16),
              waits=[tok_yev[(eix, j, 0)], tok_yev[(eix, j, 1)]])

    w_yall = [(q_ygs[j], nyg[j]) for j in range(4)]

    nygr = [0]
    tok_pair_res = None
    tok_pair_out = None
    for pr in range(8):
        pair = (2 * pr, 2 * pr + 1)
        gw = [tok_dsi]
        if pr == 0:
            gw += list(w_yall)
        if tok_pair_res is not None:
            gw.append(tok_pair_res)
        for half, tt2 in enumerate(pair):
            nygr[0] += 16
            S("gpsimd", lambda e, tt2=tt2, half=half: e.indirect_dma_start(
                t_yg[:, half * C:(half + 1) * C], None,
                d_yg, IndirectOffsetOnAxis(ap=t_dsi[:, tt2:tt2 + 1], axis=0),
                bounds_check=NE * 512 - 1, oob_is_err=False).then_inc(
                q_ygr[0], 16), waits=(gw if half == 0 else []))
        ygw = (q_ygr[0], nygr[0])
        rw = [(q_spill, 16 * NTT)]
        if tok_pair_res is not None:
            rw.append(tok_pair_res)
        for half, tt2 in enumerate(pair):
            n_xrel2[half] += 1
            S("sync", lambda e, tt2=tt2, half=half: e.dma_start(
                t_xrel2[:, half * C:(half + 1) * C],
                d_x1[tt2 * 128:(tt2 + 1) * 128, :]).then_inc(
                q_xrel2[half], 16), waits=(rw if half == 0 else []))
        xr2w = [(q_xrel2[0], 16 * n_xrel2[0]), (q_xrel2[1], 16 * n_xrel2[1])]

        Y, XR, CT, YG = t_y2, t_xrel2, t_ct2, t_yg
        sc = t_st2
        Y3 = Y[:].rearrange("p (h c) -> p h c", h=2)
        XR3 = XR[:].rearrange("p (h c) -> p h c", h=2)
        CT3 = CT[:].rearrange("p (h c) -> p h c", h=2)
        YG3 = YG[:].rearrange("p (h c) -> p h c", h=2)
        bo2 = t_bo[:].rearrange("p c -> p () c").broadcast_to([128, 2, C])
        gam2 = t_gam[:].rearrange("p c -> p () c").broadcast_to([128, 2, C])
        bet2 = t_bet[:].rearrange("p c -> p () c").broadcast_to([128, 2, C])

        r0w = [ygw, w_setup] + xr2w
        if tok_pair_out is not None:
            r0w += tok_pair_out
        t_a = S("vector", lambda e: e.tensor_tensor(Y3, YG3, bo2, op=ALU.add),
                waits=r0w)
        t_r1 = S("vector", lambda e: e.tensor_tensor(Y[:], Y[:], XR[:],
                                                     op=ALU.add), waits=[t_a])

        def ln_pair(first_w, eps, affine, outp):
            tns = S("vector", lambda e: e.reduce_sum(
                sc[:, 0:2].rearrange("p h -> p h ()"), Y3, axis=AX.X,
                negate=True), waits=first_w)
            tnm = S("vector", lambda e: e.tensor_scalar(
                sc[:, 2:4], sc[:, 0:2], 1.0 / C, None, op0=ALU.mult), waits=[tns])
            tct = None
            for hh in range(2):
                tct = S("scalar", lambda e, hh=hh: e.activation(
                    CT[:, hh * C:(hh + 1) * C], Y[:, hh * C:(hh + 1) * C],
                    AF.Identity, bias=sc[:, 2 + hh:3 + hh]),
                    waits=[tnm] if hh == 0 else [])
            tsq = S("scalar", lambda e: e.activation(Y[:], CT[:], AF.Square),
                    waits=[tct])
            tvr = S("vector", lambda e: e.reduce_sum(
                sc[:, 4:6].rearrange("p h -> p h ()"), Y3, axis=AX.X),
                waits=[tsq])
            tv2 = S("vector", lambda e, eps=eps: e.tensor_scalar(
                sc[:, 6:8], sc[:, 4:6], 1.0 / C, eps, op0=ALU.mult,
                op1=ALU.add), waits=[tvr])
            tln = S("scalar", lambda e: e.activation(sc[:, 8:10], sc[:, 6:8],
                                                     AF.Ln), waits=[tv2])
            trs = S("scalar", lambda e: e.activation(sc[:, 10:12], sc[:, 8:10],
                                                     AF.Exp, scale=-0.5),
                    waits=[tln])
            if affine:
                tz = None
                for hh in range(2):
                    tz = S("scalar", lambda e, hh=hh: e.activation(
                        Y[:, hh * C:(hh + 1) * C], CT[:, hh * C:(hh + 1) * C],
                        AF.Identity, scale=sc[:, 10 + hh:11 + hh]),
                        waits=[trs] if hh == 0 else [])
                tz = S("vector", lambda e: e.tensor_tensor(Y3, Y3, gam2,
                                                           op=ALU.mult), waits=[tz])
                tz = S("vector", lambda e: e.tensor_tensor(Y3, Y3, bet2,
                                                           op=ALU.add), waits=[tz])
            else:
                tz = None
                for hh in range(2):
                    tz = S("scalar", lambda e, hh=hh: e.activation(
                        t_ct2[:, hh * C:(hh + 1) * C],
                        CT[:, hh * C:(hh + 1) * C],
                        AF.Identity, scale=sc[:, 10 + hh:11 + hh]),
                        waits=[trs] if hh == 0 else [])
            return tz

        t_z = ln_pair([t_r1], EPS2, True, None)
        t_r2 = S("vector", lambda e: e.tensor_tensor(Y[:], XR[:], Y[:],
                                                     op=ALU.add), waits=[t_z])
        tok_pair_res = t_r2
        t_o = ln_pair([t_r2], EPS1, False, CT3)

        tok_pair_out = []
        for half, tt2 in enumerate(pair):
            n_out2[half] += 16
            S("sync", lambda e, tt2=tt2, half=half: e.dma_start(
                d_out[tt2 * 128:(tt2 + 1) * 128, :],
                t_ct2[:, half * C:(half + 1) * C]).then_inc(q_out2[half], 16),
              waits=[t_o])
            tok_pair_out.append((q_out2[half], n_out2[half]))

    S("sync", lambda e: e.nop(), waits=[(q_out2[0], n_out2[0]), (q_out2[1], n_out2[1])])

    sched.finalize()
    for blk_ops in sched.blocks:
        with nc.Block() as block:
            for ename, deco in (("sync", block.sync), ("gpsimd", block.gpsimd),
                                ("scalar", block.scalar), ("vector", block.vector),
                                ("tensor", block.tensor)):
                def body(engine, ename=ename, blk_ops=blk_ops):
                    sched.emit_block(blk_ops, ename, engine, eng_sems)
                deco(body)

    b_st.close()
    st.close()
    return nc


_NC_CACHE = {}


def _get_nc():
    if "nc" not in _NC_CACHE:
        _NC_CACHE["nc"] = build_program()
    return _NC_CACHE["nc"]


def _build_masks(seq0):
    p = np.arange(128)[:, None]
    f = np.arange(W3)[None, :]
    out = np.zeros((3, 128, W3), dtype=np.float32)
    for v in range(3):
        tt = v if v < 2 else 2
        allowed = (f >= p) & (f <= p + 256)
        if v < 2:
            kseq = seq0 - 256 + tt * 128 + f
            allowed = allowed & (kseq >= 0)
        out[v] = np.where(allowed, 0.0, NEG)
    return out


def kernel(L0_attn_x, L0_attn_W_attn, L0_attn_b_attn, L0_attn_W_proj, L0_attn_b_proj,
           L0_ffn_Wr, L0_ffn_W1, L0_ffn_b1, L0_ffn_Wo, L0_ffn_bo, L0_ffn_gamma,
           L0_ffn_beta):
    x = np.ascontiguousarray(np.asarray(L0_attn_x, dtype=np.float32).reshape(B * T, C))
    wa = np.ascontiguousarray(np.asarray(L0_attn_W_attn, dtype=np.float32))
    ba = np.ascontiguousarray(np.asarray(L0_attn_b_attn, dtype=np.float32))
    wp = np.ascontiguousarray(np.asarray(L0_attn_W_proj, dtype=np.float32))
    bp = np.ascontiguousarray(np.asarray(L0_attn_b_proj, dtype=np.float32))
    wrt = np.ascontiguousarray(np.asarray(L0_ffn_Wr, dtype=np.float32).T)
    w1 = np.ascontiguousarray(np.asarray(L0_ffn_W1, dtype=np.float32))
    b1 = np.ascontiguousarray(np.asarray(L0_ffn_b1, dtype=np.float32))
    wo = np.ascontiguousarray(np.asarray(L0_ffn_Wo, dtype=np.float32))
    bo = np.ascontiguousarray(np.asarray(L0_ffn_bo, dtype=np.float32))
    gam = np.ascontiguousarray(np.asarray(L0_ffn_gamma, dtype=np.float32))
    bet = np.ascontiguousarray(np.asarray(L0_ffn_beta, dtype=np.float32))

    in_maps = []
    for c in range(NCORES):
        base = c * TCORE
        seq0 = (c % 4) * TCORE
        xe = np.zeros((NTQ * 128, C), dtype=np.float32)
        if seq0 == 0:
            xe[HALO:] = x[base:base + TCORE]
        else:
            xe[:] = x[base - HALO:base + TCORE]
        in_maps.append({
            "x_ext": xe, "w_attn": wa, "b_attn": ba, "w_proj": wp, "b_proj": bp,
            "wr_t": wrt, "w1": w1, "b1": b1, "wo": wo, "bo": bo, "gamma": gam,
            "beta": bet, "masks": _build_masks(seq0),
            "eids": np.arange(NE, dtype=np.float32).reshape(NE, 1),
            "ones8": np.ones((NE, 1), dtype=np.float32),
        })

    nc = _get_nc()
    res = run_bass_kernel_spmd(nc, in_maps, core_ids=list(range(NCORES)))
    out = np.concatenate([r["out"] for r in res.results], axis=0)
    return out.reshape(B, T, C)


# version 16
# speedup vs baseline: 1.0359x; 1.0359x over previous
import numpy as np
import ml_dtypes
from contextlib import ExitStack

import concourse.bass as bass
import concourse.mybir as mybir
from concourse.bass import IndirectOffsetOnAxis
from concourse.bass_utils import run_bass_kernel_spmd

F32 = mybir.dt.float32
BF16 = mybir.dt.bfloat16
U8 = mybir.dt.uint8
U32 = mybir.dt.uint32
I32 = mybir.dt.int32
AF = mybir.ActivationFunctionType
ALU = mybir.AluOpType
AX = mybir.AxisListType

NCORES = 8
B, T, C = 2, 8192, 1024
TCORE = 2048
HALO = 256
NTT = 16
NTQ = 18
NH, HD = 16, 64
W3 = 384
FF = 4096
NE = 8
NKC = 8
NEG = -1.0e30
NSL = 8
EPS1 = 1e-5
EPS2 = 1e-12

ENGINES = ("sync", "gpsimd", "scalar", "vector", "tensor")


class Sched:

    def __init__(self):
        self.blocks = []
        self.cur = None

    def new_block(self):
        self.cur = []
        self.blocks.append(self.cur)

    def op(self, eng, fn, waits=()):
        rec = {"eng": eng, "fn": fn, "waits": [w for w in waits if w is not None],
               "inc": None}
        self.cur.append(rec)
        return rec

    def finalize(self):
        for blk in self.blocks:
            for rec in blk:
                for wtg in rec["waits"]:
                    if isinstance(wtg, dict) and wtg["inc"] is None:
                        wtg["inc"] = -1
        counters = {e: 0 for e in ENGINES}
        for blk in self.blocks:
            for rec in blk:
                if rec["inc"] == -1:
                    counters[rec["eng"]] += 1
                    rec["inc"] = counters[rec["eng"]]

    def emit_block(self, blk_ops, eng_name, engine, eng_sems):
        highwater = {}
        for rec in blk_ops:
            if rec["eng"] != eng_name:
                continue
            for wtg in rec["waits"]:
                if isinstance(wtg, dict):
                    sem, val, key = eng_sems[wtg["eng"]], wtg["inc"], ("e", wtg["eng"])
                else:
                    sem, val = wtg
                    key = ("s", id(sem))
                if highwater.get(key, -1) >= val:
                    continue
                highwater[key] = val
                engine.wait_ge(sem, val)
            inst = rec["fn"](engine)
            if rec["inc"] is not None:
                assert inst is not None, f"wait-target op returned no inst on {eng_name}"
                inst.then_inc(eng_sems[rec["eng"]], 1)


def build_program():
    nc = bass.Bass("TRN2", target_bir_lowering=False)

    d_x = nc.dram_tensor("x_ext", [NTQ * 128, C], F32, kind="ExternalInput").ap()
    d_wa = nc.dram_tensor("w_attn", [C, 3 * C], F32, kind="ExternalInput").ap()
    d_ba = nc.dram_tensor("b_attn", [3 * C], F32, kind="ExternalInput").ap()
    d_wp = nc.dram_tensor("w_proj", [C, C], F32, kind="ExternalInput").ap()
    d_bp = nc.dram_tensor("b_proj", [C], F32, kind="ExternalInput").ap()
    d_wrt = nc.dram_tensor("wr_t", [C, NE], F32, kind="ExternalInput").ap()
    d_w1 = nc.dram_tensor("w1", [NE, C, FF], F32, kind="ExternalInput").ap()
    d_b1 = nc.dram_tensor("b1", [NE, FF], F32, kind="ExternalInput").ap()
    d_wo = nc.dram_tensor("wo", [FF, C], F32, kind="ExternalInput").ap()
    d_bo = nc.dram_tensor("bo", [C], F32, kind="ExternalInput").ap()
    d_gam = nc.dram_tensor("gamma", [C], F32, kind="ExternalInput").ap()
    d_bet = nc.dram_tensor("beta", [C], F32, kind="ExternalInput").ap()
    d_msk = nc.dram_tensor("masks", [3, 128, W3], F32, kind="ExternalInput").ap()
    d_out = nc.dram_tensor("out", [TCORE, C], F32, kind="ExternalOutput").ap()
    d_x1 = nc.dram_tensor("x1_scratch", [TCORE, C], F32, kind="Internal").ap()
    d_idx = nc.dram_tensor("idx_scratch", [TCORE], F32, kind="Internal").ap()
    d_eids = nc.dram_tensor("eids", [NE, 1], F32, kind="ExternalInput").ap()
    d_ones8 = nc.dram_tensor("ones8", [NE, 1], F32, kind="ExternalInput").ap()
    d_xg = nc.dram_tensor("xg_scratch", [NE * 512, C], BF16, kind="Internal").ap()
    d_yg = nc.dram_tensor("yg_scratch", [NE * 512, C], F32, kind="Internal").ap()
    d_dst = nc.dram_tensor("dst_scratch", [TCORE], F32, kind="Internal").ap()
    d_wob = nc.dram_tensor("wo_bf", [FF, C], BF16, kind="Internal").ap()

    ident_bf = nc.inline_tensor(np.eye(128, dtype=ml_dtypes.bfloat16), "ident_bf").ap()
    ident_f32 = nc.inline_tensor(np.eye(128, dtype=np.float32), "ident_f32").ap()

    st = ExitStack()
    sb = lambda name, shape, dt: st.enter_context(nc.sbuf_tensor(name, shape, dt))
    psal = lambda name, shape, dt: st.enter_context(nc.psum_tensor(name, shape, dt))
    sem = lambda name: st.enter_context(nc.semaphore(name))

    t_identb = sb("identb", [128, 128], BF16)
    t_identf = sb("identf", [128, 128], F32)
    t_bqk = sb("bqk", [128, 16], F32)
    t_bqs = sb("bqs", [128, 8], F32)
    t_wrt = sb("wrt", [128, NKC * NE], F32)
    t_b1c = sb("b1cols", [128, NE * 32], F32)
    t_eids = sb("eids_sb", [NE, 1], F32)
    t_ones8 = sb("ones8_sb", [NE, 1], F32)
    t_ebase = sb("ebase_sb", [NE, 1], F32)
    t_bo = sb("bo_bc", [128, C], F32)
    t_gam = sb("gam_bc", [128, C], F32)
    t_bet = sb("bet_bc", [128, C], F32)

    eng_sems = {e: sem("e_" + e) for e in ENGINES}
    q_setup = sem("q_setup")
    q_wsetup = sem("q_wsetup")
    q_wk = sem("q_wk")
    q_wv = sem("q_wv")
    q_wq = sem("q_wq")
    q_x = [sem(f"q_x{i}") for i in range(6)]
    q_spill = sem("q_spill")
    q_idx = sem("q_idx")
    q_idxbc = sem("q_idxbc")
    q_b1c = sem("q_b1c")
    q_gsetup = sem("q_gsetup")
    q_xrel = [sem(f"q_xrel{i}") for i in range(4)]
    q_xrel2 = [sem(f"q_xrel2{i}") for i in range(4)]
    q_slot = [sem(f"q_slot{i}") for i in range(32)]
    q_out2 = [sem("q_out0"), sem("q_out1")]
    q_xg2 = [sem("q_xg0"), sem("q_xg1")]
    q_wst = [sem("q_wst0"), sem("q_wst1")]
    q_zg = sem("q_zg")
    q_dst = sem("q_dst")
    q_dst2 = sem("q_dst2")
    q_wob = sem("q_wob")
    q_xgl = [sem("q_xgl0"), sem("q_xgl1")]
    q_ygs = [sem(f"q_ygs{i}") for i in range(4)]
    q_ygr = [sem("q_ygr0"), sem("q_ygr1")]

    sched = Sched()

    a_st = ExitStack()
    asb = lambda name, shape, dt: a_st.enter_context(nc.sbuf_tensor(name, shape, dt))

    t_wab = asb("wab", [128, NKC * 3 * C], BF16)
    t_wpb = asb("wpb", [128, NKC * C], BF16)
    t_kring = asb("kring", [128, NSL * C], BF16)
    t_vring = asb("vring", [128, NSL * C], BF16)
    t_masks = asb("masks_sb", [128, 3 * W3], BF16)
    t_bvbc = asb("bv_bc", [128, C], BF16)
    t_bpbc = asb("bp_bc", [128, C], BF16)
    t_xc = asb("xc", [128, 2 * C], BF16)
    t_xTb4 = asb("xTb4", [128, NKC * 512], BF16)
    t_q4 = asb("q4", [128, NKC * 512], BF16)
    stream = asb("stream", [128, NSL * C], F32)
    t_scf = asb("scf", [128, 4 * W3], F32)
    t_prob = asb("prob", [128, 2 * 4 * W3], BF16)
    t_probT = asb("probT", [128, 2 * 4 * W3], BF16)
    t_ao = asb("attn_out", [128, C], BF16)
    t_aon = asb("attn_out_n", [128, C], BF16)
    t_aoT = asb("attn_outT", [128, C], BF16)
    t_den = asb("den", [128, NH], F32)
    t_rden = asb("rden", [128, NH], F32)
    t_w = asb("wbuf", [128, C], F32)
    t_cent = asb("centbuf", [128, C], F32)
    t_xT32 = asb("xT32", [128, C], F32)
    t_lg = asb("lgs", [128, 8], F32)
    t_mx8 = asb("mx8", [128, 8], F32)
    t_idxu = asb("idxu", [128, 8], U32)
    t_idxf = asb("idxf", [128, 1], F32)
    t_stat = asb("stat", [128, 8], F32)

    apsal = lambda name, shape, dt: a_st.enter_context(nc.psum_tensor(name, shape, dt))
    p_tra = apsal("p_tra", [128, 1024], BF16)
    p_trb = apsal("p_trb", [128, 1024], BF16)
    p_v = apsal("p_v", [128, 512], F32)
    p_sc4 = [apsal(f"p_sc4_{i}", [128, 512], F32) for i in range(4)]
    p_misc = apsal("p_misc", [128, 512], F32)

    sched.new_block()
    S = sched.op

    n_setup = [0]
    n_gsetup = [0]

    def setup_dma(engname, fn):
        if engname == "gpsimd":
            S(engname, lambda e, fn=fn: fn(e).then_inc(q_gsetup, 16))
            n_gsetup[0] += 16
        else:
            S(engname, lambda e, fn=fn: fn(e).then_inc(q_setup, 16))
            n_setup[0] += 16

    setup_dma("sync", lambda e: e.dma_start(t_identb[:], ident_bf))
    setup_dma("sync", lambda e: e.dma_start(t_identf[:], ident_f32))
    for v in range(3):
        setup_dma("gpsimd", lambda e, v=v: e.dma_start(
            t_masks[:, v * W3:(v + 1) * W3], d_msk[v]))
    setup_dma("sync", lambda e: e.dma_start(
        t_bqk[:], bass.AP(d_ba.tensor, 0, [[1, 128], [128, 16]]),
        allow_slow_non_contiguous=True))

    def bcast_src(dap, off, n):
        return bass.AP(dap.tensor, off, [[0, 128], [1, n]])

    setup_dma("gpsimd", lambda e: e.dma_start(t_bvbc[:], bcast_src(d_ba, 2048, C)))
    setup_dma("gpsimd", lambda e: e.dma_start(t_bpbc[:], bcast_src(d_bp, 0, C)))
    setup_dma("sync", lambda e: e.dma_start(t_bo[:], bcast_src(d_bo, 0, C)))
    setup_dma("sync", lambda e: e.dma_start(t_gam[:], bcast_src(d_gam, 0, C)))
    setup_dma("sync", lambda e: e.dma_start(t_bet[:], bcast_src(d_bet, 0, C)))
    for kc in range(NKC):
        setup_dma("sync", lambda e, kc=kc: e.dma_start(
            t_wrt[:, kc * NE:(kc + 1) * NE], d_wrt[kc * 128:(kc + 1) * 128, :]))
    setup_dma("sync", lambda e: e.dma_start(t_eids[:], d_eids))
    setup_dma("sync", lambda e: e.dma_start(t_ones8[:], d_ones8))
    SETUP_TOTAL = n_setup[0]



    nws = [0, 0]
    last_cast = [None, None]
    w_wk = None
    w_wkv = None
    for si in range(32):
        koff = C if si < 16 else 2 * C
        kc, h = (si % 16) // 2, si % 2
        sb2 = si % 2
        lw = [last_cast[sb2]] if last_cast[sb2] is not None else []
        srcap = d_wa[kc * 128:(kc + 1) * 128,
                     koff + h * 512: koff + (h + 1) * 512]
        nws[sb2] += 16
        S("scalar", lambda e, srcap=srcap, sb2=sb2: e.dma_start(
            t_w[:, sb2 * 512:(sb2 + 1) * 512], srcap).then_inc(q_wst[sb2], 16),
          waits=lw)
        wsw = (q_wst[sb2], nws[sb2])
        dst = t_wab[:, kc * 3 * C + koff + h * 512:
                    kc * 3 * C + koff + (h + 1) * 512]
        last_cast[sb2] = S("vector", lambda e, dst=dst, sb2=sb2: e.tensor_copy(
            dst, t_w[:, sb2 * 512:(sb2 + 1) * 512]), waits=[wsw])
        if si == 15:
            w_wk = last_cast[sb2]
        elif si == 31:
            w_wkv = last_cast[sb2]
    for kc in range(NKC):
        S("gpsimd", lambda e, kc=kc: e.dma_start(
            t_wab[:, kc * 3 * C: kc * 3 * C + C],
            d_wa[kc * 128:(kc + 1) * 128, 0:C]).then_inc(q_wq, 16))
    w_wkvq = (q_wq, 8 * 16)
    for kc in range(NKC):
        S("gpsimd", lambda e, kc=kc: e.dma_start(
            t_wpb[:, kc * C:(kc + 1) * C],
            d_wp[kc * 128:(kc + 1) * 128, :]).then_inc(q_wsetup, 16))
    w_wsetup = (q_wsetup, 8 * 16)
    n_b1c = [0]
    for j in range(32):
        S("scalar", lambda e, j=j: e.dma_start(
            t_b1c[:, j * 8:(j + 1) * 8],
            bass.AP(d_b1.tensor, j * 128, [[1, 128], [FF, 8]]),
            allow_slow_non_contiguous=True).then_inc(q_b1c, 16))
        n_b1c[0] += 16
    w_b1c = (q_b1c, n_b1c[0])
    S("gpsimd", lambda e: e.dma_start(d_wob, d_wo).then_inc(q_wob, 16))
    w_setup = (q_setup, SETUP_TOTAL)
    w_gsetup = (q_gsetup, n_gsetup[0])

    tok_bqs = S("vector", lambda e: e.tensor_scalar(
        t_bqs[:], t_bqk[:, 0:8], 0.125, None, op0=ALU.mult), waits=[w_setup])

    trb_ev = [None, None]

    tok_cast = {}
    tok_trE = {}
    tok_kev = {}
    tok_vev = {}
    tok_qev = {}
    tok_kmm = {}
    tok_qmm = {}
    tok_v_mm = {}
    tok_scg = {}
    tok_scg_h = {}
    tok_mask = {}
    tok_mask_h = {}
    tok_exp = {}
    tok_red = {}
    tok_recip = {}
    tok_ptpe = {}
    tok_ptev = {}
    tok_av = {}
    tok_aoev = {}
    tok_normao = {}
    tok_aoT_pe = {}
    tok_aoT_ev = {}
    tok_y1_mm = {}
    tok_proj_ev = {}
    tok_logit_ev = {}
    tok_xnew = {}
    tok_trP = {}

    nx_load = [0] * 6
    n_spill = [0]
    avq = {}
    chunk_tiles = [list(range(4 * c2, min(4 * c2 + 4, NTQ))) for c2 in range(5)]

    for ch, tiles in enumerate(chunk_tiles):
        Wc = len(tiles) * 128
        qoff = 256 if ch == 0 else 0

        for ti, tq in enumerate(tiles):
            sl = tq % NSL
            qi = tq % 6
            nx_load[qi] += 16
            xw = (q_x[qi], nx_load[qi])
            dst = stream[:, sl * C:(sl + 1) * C]
            lw = []
            if tq - NSL >= 2:
                tt_old = tq - NSL - 2
                lw.append((q_spill, 16 * n_spill[0]))
                lw.append(tok_logit_ev[tt_old])
            elif tq - NSL >= 0:
                lw.append(tok_cast[tq - NSL])
            S("sync", lambda e, dst=dst, tq=tq, qi=qi: e.dma_start(
                dst, d_x[tq * 128:(tq + 1) * 128, :]).then_inc(q_x[qi], 16), waits=lw)

            cbuf = t_xc[:, (tq % 2) * C:(tq % 2 + 1) * C]
            cw = [xw]
            if tq - 2 >= 0:
                cw.append(tok_trP[tq - 2])
            tok_cast[tq] = S("scalar", lambda e, cbuf=cbuf, dst=dst: e.copy(cbuf, dst),
                             waits=cw)

            def mk_tr(eng, cbuf=cbuf):
                last = None
                for j in range(NKC):
                    last = eng.transpose(p_tra[:, j * 128:(j + 1) * 128],
                                         cbuf[:, j * 128:(j + 1) * 128], t_identb[:])
                return last
            trw = [tok_cast[tq], w_setup]
            if tq >= 1:
                trw.append(tok_trE[tq - 1])
            tok_trP[tq] = S("tensor", mk_tr, waits=trw)

            xdst = (t_xTb4[:].rearrange("p (kc t) -> p kc t", kc=NKC)
                    [:, :, ti * 128:(ti + 1) * 128])
            evw = [tok_trP[tq]]
            if ch >= 1 and ti == 0:
                evw.append(tok_qmm[(ch - 1, NKC - 1)])
            tok_trE[tq] = S("vector", lambda e, xdst=xdst: e.tensor_copy(
                xdst, p_tra[:, 0:C].rearrange("p (kc t) -> p kc t", kc=NKC)),
                waits=evw)

        for fc in range(NKC):
            def mk_k(eng, fc=fc, Wc=Wc):
                last = None
                for kc in range(NKC):
                    last = eng.matmul(
                        p_sc4[fc % 4][:, 0:Wc],
                        t_wab[:, kc * 3 * C + C + fc * 128: kc * 3 * C + C + (fc + 1) * 128],
                        t_xTb4[:, kc * 512:kc * 512 + Wc],
                        start=(kc == 0), stop=(kc == NKC - 1))
                return last
            kw = [w_wk]
            if fc == 0:
                kw += [tok_trE[t2] for t2 in tiles]
                if ch >= 1:
                    lastqb_prev = 3 if ch == 1 else 4 * (ch - 1) + 3
                    kw.append(tok_mask[(lastqb_prev - 2, 3)])
            elif fc == 4:
                kw.append(tok_kev[(ch, 3)])
            tok_kmm[(ch, fc)] = S("tensor", mk_k, waits=kw)
            if fc in (3, 7):
                for fc2 in range(fc - 3, fc + 1):
                    s0 = tiles[0] % NSL
                    kdst = (t_kring[:].rearrange("p (s c) -> p s c", s=NSL)
                            [:, s0:s0 + len(tiles), fc2 * 128:(fc2 + 1) * 128])
                    ksrc = (p_sc4[fc2 % 4][:, 0:Wc]
                            .rearrange("p (t c) -> p t c", c=128))
                    tok_kev[(ch, fc2)] = S("scalar", lambda e, kdst=kdst, ksrc=ksrc,
                                           fc2=fc2:
                                           e.activation(kdst, ksrc, AF.Identity,
                                                        bias=t_bqk[:, 8 + fc2:9 + fc2]),
                                           waits=[tok_kmm[(ch, fc)], w_setup])

        for ti, tq in enumerate(tiles):
            for vh in range(2):
                def mk_v(eng, vh=vh, ti=ti):
                    last = None
                    for kc in range(NKC):
                        last = eng.matmul(
                            p_v[:],
                            t_xTb4[:, kc * 512 + ti * 128: kc * 512 + (ti + 1) * 128],
                            t_wab[:, kc * 3 * C + 2 * C + vh * 512:
                                  kc * 3 * C + 2 * C + (vh + 1) * 512],
                            start=(kc == 0), stop=(kc == NKC - 1))
                    return last
                vw = [tok_trE[tq], w_wkv]
                if vh == 1:
                    vw.append(tok_vev[(tq, 0)])
                elif (tq - 1, 1) in tok_vev:
                    vw.append(tok_vev[(tq - 1, 1)])
                tok_v_mm[(tq, vh)] = S("tensor", mk_v, waits=vw)
                vdst = t_vring[:, (tq % NSL) * C + vh * 512:
                               (tq % NSL) * C + (vh + 1) * 512]
                ew = [tok_v_mm[(tq, vh)], w_setup, w_gsetup]
                if tq - NSL in avq:
                    ew.append(avq[tq - NSL])
                tok_vev[(tq, vh)] = S("vector", lambda e, vdst=vdst, vh=vh:
                                      e.scalar_tensor_tensor(
                                          vdst, p_v[:], 1.0,
                                          t_bvbc[:, vh * 512:(vh + 1) * 512],
                                          op0=ALU.mult, op1=ALU.add), waits=ew)

        for fc in range(NKC):
            def mk_q(eng, fc=fc, Wc=Wc, qoff=qoff):
                last = None
                for kc in range(NKC):
                    last = eng.matmul(
                        p_sc4[fc % 4][:, qoff:Wc],
                        t_wab[:, kc * 3 * C + fc * 128: kc * 3 * C + (fc + 1) * 128],
                        t_xTb4[:, kc * 512 + qoff:kc * 512 + Wc],
                        start=(kc == 0), stop=(kc == NKC - 1))
                return last
            qw = [tok_bqs, w_wkvq]
            if fc == 0:
                qw.append(tok_kev[(ch, NKC - 1)])
            elif fc == 4:
                qw.append(tok_qev[(ch, 3)])
            tok_qmm[(ch, fc)] = S("tensor", mk_q, waits=qw)
            if fc in (3, 7):
                for fc2 in range(fc - 3, fc + 1):
                    qdst = t_q4[:, fc2 * 512 + qoff: fc2 * 512 + Wc]
                    tok_qev[(ch, fc2)] = S("scalar", lambda e, qdst=qdst, fc2=fc2,
                                           Wc=Wc, qoff=qoff:
                                           e.activation(qdst,
                                                        p_sc4[fc2 % 4][:, qoff:Wc],
                                                        AF.Identity, scale=0.125,
                                                        bias=t_bqs[:, fc2:fc2 + 1]),
                                           waits=[tok_qmm[(ch, fc)]])

        qbs = [2, 3] if ch == 0 else list(tiles)
        for tq in qbs:
            tt = tq - 2
            ti = tq - 4 * ch
            mvar = tt if tt < 2 else 2
            for s in range(5):
                if s < 4:
                    G = s
                    g2 = G % 2
                    for hloc in range(4):
                        def mk_sc(eng, G=G, tq=tq, ti=ti, hloc=hloc):
                            last = None
                            h = G * 4 + hloc
                            fc, r0 = h // 2, (h % 2) * 64
                            for kr in range(3):
                                ks = (tq - 2 + kr) % NSL
                                last = eng.matmul(
                                    p_sc4[hloc][:, kr * 128:(kr + 1) * 128],
                                    t_q4[r0:r0 + 64, fc * 512 + ti * 128:
                                         fc * 512 + (ti + 1) * 128],
                                    t_kring[r0:r0 + 64,
                                            ks * C + fc * 128: ks * C + (fc + 1) * 128],
                                    start=True, stop=True)
                            return last
                        sw = []
                        if G == 0 and tq == (2 if ch == 0 else tiles[0]):
                            sw.append(tok_qev[(ch, NKC - 1)])
                        pmk = (tt, G - 1, hloc) if G >= 1 else (tt - 1, 3, hloc)
                        if pmk in tok_mask_h:
                            sw.append(tok_mask_h[pmk])
                        tok_scg_h[(tt, G, hloc)] = S("tensor", mk_sc, waits=sw)
                    tok_scg[(tt, G)] = tok_scg_h[(tt, G, 3)]

                    for hloc in range(4):
                        mwh = [tok_scg_h[(tt, G, hloc)]]
                        if hloc == 0:
                            mwh += [w_setup, w_gsetup]
                            if G >= 1:
                                mwh.append(tok_exp[(tt, G - 1)])
                            elif tt >= 1:
                                mwh.append(tok_exp[(tt - 1, 3)])
                        tok_mask_h[(tt, G, hloc)] = S(
                            "vector", lambda e, hloc=hloc, mvar=mvar:
                            e.tensor_tensor(
                                t_scf[:, hloc * W3:(hloc + 1) * W3],
                                p_sc4[hloc][:, 0:W3],
                                t_masks[:, mvar * W3:(mvar + 1) * W3], op=ALU.add),
                            waits=mwh)
                    tok_mask[(tt, G)] = tok_mask_h[(tt, G, 3)]

                    pbuf = t_prob[:, g2 * 4 * W3:(g2 + 1) * 4 * W3]
                    pk = (tt, G - 2) if G >= 2 else (tt - 1, G + 2)
                    ew = [tok_mask[(tt, G)]]
                    if pk in tok_red:
                        ew += [tok_red[pk], tok_ptpe[pk + (1,)]]
                    tok_exp[(tt, G)] = S("scalar", lambda e, pbuf=pbuf: e.activation(
                        pbuf, t_scf[:], AF.Exp), waits=ew)

                if s >= 1:
                    G = s - 1
                    g2 = G % 2
                    pbuf = t_prob[:, g2 * 4 * W3:(g2 + 1) * 4 * W3]
                    for sub in range(2):
                        def mk_pt(eng, pbuf=pbuf, sub=sub):
                            last = None
                            for hl in range(2):
                                for kr in range(3):
                                    last = eng.transpose(
                                        p_trb[:, (hl * 3 + kr) * 128:
                                              (hl * 3 + kr + 1) * 128],
                                        pbuf[:, (2 * sub + hl) * W3 + kr * 128:
                                             (2 * sub + hl) * W3 + (kr + 1) * 128],
                                        t_identb[:])
                            return last
                        tok_ptpe[(tt, G, sub)] = S("tensor", mk_pt,
                                                   waits=[tok_exp[(tt, G)], trb_ev[1]])
                        ptdst = t_probT[:, g2 * 4 * W3 + sub * 768:
                                        g2 * 4 * W3 + (sub + 1) * 768]
                        ew = [tok_ptpe[(tt, G, sub)]]
                        ak = ((tt, G - 2, 2 * sub + 1) if G >= 2
                              else (tt - 1, G + 2, 2 * sub + 1))
                        if ak in tok_av:
                            ew.append(tok_av[ak])
                        tok_ptev[(tt, G, sub)] = S("vector",
                                                   lambda e, ptdst=ptdst: e.tensor_copy(
                                                       ptdst, p_trb[:, 0:768]), waits=ew)
                        trb_ev[1] = tok_ptev[(tt, G, sub)]

                    for hloc in range(4):
                        h = G * 4 + hloc
                        sub = hloc // 2

                        def mk_av(eng, g2=g2, hloc=hloc, h=h, G=G, tq=tq):
                            last = None
                            for kr in range(3):
                                ks = (tq - 2 + kr) % NSL
                                last = eng.matmul(
                                    p_misc[:, (G % 2) * 256 + hloc * 64:
                                           (G % 2) * 256 + (hloc + 1) * 64],
                                    t_probT[:, g2 * 4 * W3 + hloc * W3 + kr * 128:
                                            g2 * 4 * W3 + hloc * W3 + (kr + 1) * 128],
                                    t_vring[:, ks * C + h * 64: ks * C + (h + 1) * 64],
                                    start=(kr == 0), stop=(kr == 2))
                            return last
                        aw = [tok_ptev[(tt, G, sub)], tok_vev[(tq, 1)]]
                        if G == 0 and hloc == 0 and tt >= 1:
                            aw.append(tok_logit_ev[tt - 1])
                        pk2 = (tt, G - 1) if G >= 1 else (tt - 1, 3)
                        if hloc == 0 and pk2 in tok_aoev:
                            aw.append(tok_aoev[pk2])
                        tok_av[(tt, G, hloc)] = S("tensor", mk_av, waits=aw)

                    tok_aoev[(tt, G)] = S("vector", lambda e, G=G: e.tensor_copy(
                        t_ao[:, G * 256:(G + 1) * 256],
                        p_misc[:, (G % 2) * 256:(G % 2 + 1) * 256]),
                        waits=[tok_av[(tt, G, 3)]])

                    tok_red[(tt, G)] = S("vector", lambda e, G=G, pbuf=pbuf:
                        e.reduce_sum(
                            t_den[:, G * 4:(G + 1) * 4].rearrange("p h -> p h ()"),
                            pbuf.rearrange("p (h j) -> p h j", h=4), axis=AX.X),
                        waits=[tok_exp[(tt, G)]])

            avq[tt] = tok_av[(tt, 3, 3)]

            tok_recip[tt] = S("vector", lambda e: e.reciprocal(t_rden[:], t_den[:]),
                              waits=[tok_red[(tt, 3)]])
            nw = [tok_recip[tt], tok_aoev[(tt, 3)]]
            if tt >= 1:
                nw.append(tok_aoT_pe[tt - 1])
            tok_normao[tt] = S("vector", lambda e: e.tensor_tensor(
                t_aon[:].rearrange("p (h d) -> p h d", h=16),
                t_ao[:].rearrange("p (h d) -> p h d", h=16),
                t_rden[:].rearrange("p h -> p h ()").broadcast_to([128, NH, HD]),
                op=ALU.mult), waits=nw)

            def mk_aoT(eng):
                last = None
                for j in range(NKC):
                    last = eng.transpose(p_trb[:, j * 128:(j + 1) * 128],
                                         t_aon[:, j * 128:(j + 1) * 128], t_identb[:])
                return last
            aw = [tok_normao[tt], trb_ev[1]]
            tok_aoT_pe[tt] = S("tensor", mk_aoT, waits=aw)
            ew = [tok_aoT_pe[tt]]
            if tt >= 1:
                ew.append(tok_y1_mm[(tt - 1, 1)])
            tok_aoT_ev[tt] = S("vector", lambda e: e.tensor_copy(t_aoT[:], p_trb[:, 0:C]),
                               waits=ew)
            trb_ev[1] = tok_aoT_ev[tt]

            for nh2 in range(2):
                def mk_pj(eng, nh2=nh2):
                    last = None
                    for kc in range(NKC):
                        last = eng.matmul(
                            p_misc[:],
                            t_aoT[:, kc * 128:(kc + 1) * 128],
                            t_wpb[:, kc * C + nh2 * 512: kc * C + (nh2 + 1) * 512],
                            start=(kc == 0), stop=(kc == NKC - 1))
                    return last
                pw = [tok_aoT_ev[tt], w_wsetup]
                if nh2 == 1:
                    pw.append(tok_proj_ev[(tt, 0)])
                tok_y1_mm[(tt, nh2)] = S("tensor", mk_pj, waits=pw)
                ew = [tok_y1_mm[(tt, nh2)], w_setup, w_gsetup]
                tok_proj_ev[(tt, nh2)] = S("vector", lambda e, nh2=nh2:
                                           e.scalar_tensor_tensor(
                                               t_w[:, nh2 * 512:(nh2 + 1) * 512], p_misc[:], 1.0,
                                               t_bpbc[:, nh2 * 512:(nh2 + 1) * 512],
                                               op0=ALU.mult, op1=ALU.add), waits=ew)

            tok_res = S("vector", lambda e, tq=tq: e.tensor_tensor(
                t_w[:], t_w[:], stream[:, (tq % NSL) * C:(tq % NSL + 1) * C], op=ALU.add),
                waits=[tok_proj_ev[(tt, 0)], tok_proj_ev[(tt, 1)]])

            tok_ns = S("vector", lambda e: e.reduce_sum(t_stat[:, 0:1], t_w[:], axis=AX.X,
                                                        negate=True), waits=[tok_res])
            tok_nm = S("vector", lambda e: e.tensor_scalar(t_stat[:, 1:2], t_stat[:, 0:1],
                                                           1.0 / C, None, op0=ALU.mult),
                       waits=[tok_ns])
            tok_ct = S("vector", lambda e: e.tensor_scalar(t_cent[:], t_w[:], t_stat[:, 1:2],
                                                           None, op0=ALU.add), waits=[tok_nm])
            tok_sq = S("vector", lambda e: e.tensor_tensor(t_w[:], t_cent[:], t_cent[:],
                                                           op=ALU.mult), waits=[tok_ct])
            tok_vr = S("vector", lambda e: e.reduce_sum(t_stat[:, 2:3], t_w[:], axis=AX.X),
                       waits=[tok_sq])
            tok_v2 = S("vector", lambda e: e.tensor_scalar(t_stat[:, 3:4], t_stat[:, 2:3],
                                                           1.0 / C, EPS1, op0=ALU.mult,
                                                           op1=ALU.add), waits=[tok_vr])
            tok_ln = S("scalar", lambda e: e.activation(t_stat[:, 4:5], t_stat[:, 3:4], AF.Ln),
                       waits=[tok_v2])
            tok_rs = S("scalar", lambda e: e.activation(t_stat[:, 5:6], t_stat[:, 4:5], AF.Exp,
                                                        scale=-0.5), waits=[tok_ln])
            tok_xnew[tt] = S("vector", lambda e, tq=tq: e.tensor_scalar(
                stream[:, (tq % NSL) * C:(tq % NSL + 1) * C], t_cent[:], t_stat[:, 5:6],
                None, op0=ALU.mult), waits=[tok_rs])

            n_spill[0] += 1
            S("sync", lambda e, tt=tt, tq=tq: e.dma_start(
                d_x1[tt * 128:(tt + 1) * 128, :],
                stream[:, (tq % NSL) * C:(tq % NSL + 1) * C]).then_inc(q_spill, 16),
              waits=[tok_xnew[tt]])

            tok_rtr_ev = {}
            for bt in range(2):
                def mk_rtr(eng, bt=bt, tq=tq):
                    last = None
                    for j in range(4):
                        last = eng.transpose(
                            p_misc[:, j * 128:(j + 1) * 128],
                            stream[:, (tq % NSL) * C + (bt * 4 + j) * 128: (tq % NSL) * C + (bt * 4 + j + 1) * 128],
                            t_identf[:])
                    return last
                rw = [tok_xnew[tt]]
                rw.append(tok_proj_ev[(tt, 1)] if bt == 0 else tok_rtr_ev[0])
                tok_rtr_pe = S("tensor", mk_rtr, waits=rw)
                tok_rtr_ev[bt] = S("vector", lambda e, bt=bt: e.tensor_copy(
                    t_xT32[:, bt * 512:(bt + 1) * 512], p_misc[:]), waits=[tok_rtr_pe])

            def mk_rmm(eng):
                last = None
                for kc in range(NKC):
                    last = eng.matmul(p_misc[:, 0:NE],
                                      t_xT32[:, kc * 128:(kc + 1) * 128],
                                      t_wrt[:, kc * NE:(kc + 1) * NE],
                                      start=(kc == 0), stop=(kc == NKC - 1))
                return last
            tok_rmm = S("tensor", mk_rmm, waits=[tok_rtr_ev[0], tok_rtr_ev[1], w_setup])
            tok_logit_ev[tt] = S("vector", lambda e: e.tensor_copy(t_lg[:], p_misc[:, 0:NE]),
                                 waits=[tok_rmm])
            tok_m8 = S("vector", lambda e: e.max(t_mx8[:], t_lg[:]), waits=[tok_logit_ev[tt]])
            tok_mi = S("vector", lambda e: e.max_index(t_idxu[:], t_mx8[:], t_lg[:]),
                       waits=[tok_m8])
            tok_if = S("vector", lambda e: e.tensor_scalar(t_idxf[:], t_idxu[:, 0:1], 0.0,
                                                           None, op0=ALU.add),
                       waits=[tok_mi] + ([(q_idx, 16 * tt)] if tt >= 1 else []))
            S("sync", lambda e, tt=tt: e.dma_start(
                d_idx.rearrange("(a p) -> a p", p=128)[tt].rearrange("(p o) -> p o", o=1),
                t_idxf[:]).then_inc(q_idx, 16), waits=[tok_if])

    S("sync", lambda e: e.nop(), waits=[(q_spill, 16 * NTT), (q_idx, 16 * NTT)])

    a_st.close()
    b_st = ExitStack()
    bsb = lambda name, shape, dt: b_st.enter_context(nc.sbuf_tensor(name, shape, dt))

    t_ring = bsb("ring", [128, 32 * 1024], BF16)
    t_idxbc = bsb("idx_bc", [128, 2048], F32)
    t_rank = bsb("rank", [128, 2048], F32)
    t_dsf = bsb("dsf", [128, 16], F32)
    t_dsi = bsb("dsi", [128, 16], I32)
    t_xrb = bsb("xrb", [128, 2 * C], BF16)
    t_xgl = bsb("xgl", [128, 2 * C], BF16)
    t_xgT = bsb("xgT", [128, 2 * NKC * 512], BF16)
    t_zg = bsb("zg", [128, 32 * 512], BF16)
    t_yst = bsb("yst", [128, 4 * C], F32)
    t_yg = bsb("yg", [128, 2 * C], F32)
    t_xrel2 = bsb("xrel2", [128, 2 * C], F32)
    t_y2 = bsb("y2", [128, 2 * C], F32)
    t_ct2 = bsb("ct2", [128, 2 * C], F32)
    t_st2 = bsb("stat2", [128, 16], F32)

    bpsal = lambda name, shape, dt: b_st.enter_context(nc.psum_tensor(name, shape, dt))
    p_i = [bpsal(f"p_i{j}", [128, 512], F32) for j in range(4)]
    p_y2 = [bpsal(f"p_y2{j}", [128, 512], F32) for j in range(2)]
    p_trB = bpsal("p_trB", [128, 1024], BF16)

    sched.new_block()
    S = sched.op

    nw_slot = [0] * 32
    slot_last_reader = [None] * 32
    load_ctr = [0]
    n_xrel = [0] * 4
    n_xrel2 = [0] * 4
    n_out2 = [0, 0]

    tok_idxbc = S("sync", lambda e: e.dma_start(
        t_idxbc[:], bass.AP(d_idx.tensor, 0, [[0, 128], [1, 2048]])).then_inc(
        q_idxbc, 16))
    w_idxbc = (q_idxbc, 16)

    tok_ebase = S("vector", lambda e: e.tensor_scalar(
        t_ebase[:], t_eids[:], 512.0, -1.0, op0=ALU.mult, op1=ALU.add),
        waits=[w_setup])
    tok_eq8 = S("vector", lambda e: e.tensor_scalar(
        t_rank[0:8, :], t_idxbc[0:8, :], t_eids[:], None, op0=ALU.is_equal),
        waits=[w_idxbc, w_setup])
    cur, oth = 0, 32
    tok_cs = tok_eq8
    for k in [1 << i for i in range(11)]:
        t1 = S("vector", lambda e, cur=cur, oth=oth, k=k: e.tensor_copy(
            t_rank[oth:oth + 8, 0:k], t_rank[cur:cur + 8, 0:k]), waits=[tok_cs])
        tok_cs = S("vector", lambda e, cur=cur, oth=oth, k=k: e.tensor_tensor(
            t_rank[oth:oth + 8, k:2048], t_rank[cur:cur + 8, k:2048],
            t_rank[cur:cur + 8, 0:2048 - k], op=ALU.add), waits=[t1])
        cur, oth = oth, cur
    assert cur == 32
    tok_eq8b = S("vector", lambda e: e.tensor_scalar(
        t_idxbc[0:8, :], t_idxbc[0:8, :], t_eids[:], None,
        op0=ALU.is_equal), waits=[tok_cs])
    tok_tmp = S("vector", lambda e, cur=cur: e.tensor_scalar(
        t_rank[0:8, :], t_rank[cur:cur + 8, :], t_ebase[:], None, op0=ALU.add),
        waits=[tok_cs, tok_ebase])
    tok_w8 = S("vector", lambda e: e.tensor_tensor(
        t_rank[0:8, :], t_rank[0:8, :], t_idxbc[0:8, :],
        op=ALU.mult), waits=[tok_tmp, tok_eq8b])
    w8base = 0
    tok_dsum_ev = {}
    for c4 in range(4):
        bank = p_y2[c4 % 2]
        dw = [tok_w8, w_setup]
        if c4 >= 2:
            dw.append(tok_dsum_ev[c4 - 2])
        tok_dmm = S("tensor", lambda e, bank=bank, c4=c4: e.matmul(
            bank[0:1, 0:512], t_ones8[:],
            t_rank[w8base:w8base + 8, c4 * 512:(c4 + 1) * 512],
            start=True, stop=True), waits=dw)
        tok_dsum_ev[c4] = S("vector", lambda e, bank=bank, c4=c4: e.tensor_copy(
            t_rank[64:65, c4 * 512:(c4 + 1) * 512], bank[0:1, 0:512]),
            waits=[tok_dmm])
    S("sync", lambda e: e.dma_start(d_dst, t_rank[64:65, :]).then_inc(q_dst, 16),
      waits=[tok_dsum_ev[2], tok_dsum_ev[3]])
    S("sync", lambda e: e.dma_start(
        t_dsf[:], bass.AP(d_dst.tensor, 0, [[1, 128], [128, 16]]),
        allow_slow_non_contiguous=True).then_inc(q_dst2, 16),
      waits=[(q_dst, 16)])
    tok_dsi = S("vector", lambda e: e.tensor_copy(t_dsi[:], t_dsf[:]),
                waits=[(q_dst2, 16)])

    tok_z0 = S("vector", lambda e: e.memset(t_xgl[:, 0:1024], 0.0))
    for j in range(32):
        S("sync", lambda e, j=j: e.dma_start(
            d_xg[j * 128:(j + 1) * 128, :], t_xgl[:, 0:1024]).then_inc(q_zg, 16),
          waits=[tok_z0])
    w_zg = (q_zg, 32 * 16)

    pre_slot = {}
    for iq in range(4):
        for kc in range(NKC):
            L = load_ctr[0]
            s = L % 32
            nw_slot[s] += 1
            pre_slot[(iq, kc)] = (s, nw_slot[s])
            S("gpsimd", lambda e, s=s, kc=kc, iq=iq: e.dma_start(
                t_ring[:, s * 1024:(s + 1) * 1024],
                d_w1[0, kc * 128:(kc + 1) * 128,
                     iq * 1024:(iq + 1) * 1024]).then_inc(q_slot[s], 16))
            load_ctr[0] += 1

    nscat = [0, 0]
    tok_scat = {}
    for t2 in range(NTT):
        qi = t2 % 4
        n_xrel[qi] += 1
        lw = [(q_spill, 16 * NTT)]
        if t2 - 2 >= 0:
            lw.append((q_xg2[t2 % 2], nscat[t2 % 2]))
        S("gpsimd", lambda e, t2=t2, qi=qi: e.dma_start(
            t_xrb[:, (t2 % 2) * C:(t2 % 2 + 1) * C],
            d_x1[t2 * 128:(t2 + 1) * 128, :]).then_inc(q_xrel[qi], 16), waits=lw)
        xrw = (q_xrel[qi], 16 * n_xrel[qi])
        nscat[t2 % 2] += 16
        tok_scat[t2] = S("gpsimd", lambda e, t2=t2: e.indirect_dma_start(
            d_xg, IndirectOffsetOnAxis(ap=t_dsi[:, t2:t2 + 1], axis=0),
            t_xrb[:, (t2 % 2) * C:(t2 % 2 + 1) * C], None,
            bounds_check=NE * 512 - 1, oob_is_err=False).then_inc(
            q_xg2[t2 % 2], 16), waits=[xrw, tok_dsi, w_zg])
    w_scat = [(q_xg2[0], nscat[0]), (q_xg2[1], nscat[1])]

    nxgl = [0, 0]
    tok_xgtr = {}
    tok_xgev = {}
    tok_hmm = {}
    tok_gelu = {}
    tok_womm = {}
    tok_yev = {}
    nyg = [0] * 4
    tok_hmm_last = {}
    tok_womm_last = {}
    tok_gelu_last = {}

    for eix in range(NE):
        for j in range(4):
            gb2 = (eix * 4 + j) % 2
            lw = list(w_scat) if (eix == 0 and j == 0) else []
            prev = (eix, j - 2) if j >= 2 else (eix - 1, j + 2)
            if prev in tok_xgtr:
                lw.append(tok_xgtr[prev])
            nxgl[gb2] += 16
            S("sync", lambda e, eix=eix, j=j, gb2=gb2: e.dma_start(
                t_xgl[:, gb2 * C:(gb2 + 1) * C],
                d_xg[eix * 512 + j * 128: eix * 512 + (j + 1) * 128, :]).then_inc(
                q_xgl[gb2], 16), waits=lw)
            xglw = (q_xgl[gb2], nxgl[gb2])

            def mk_xt(eng, gb2=gb2):
                last = None
                for kc in range(NKC):
                    last = eng.transpose(
                        p_trB[:, kc * 128:(kc + 1) * 128],
                        t_xgl[:, gb2 * C + kc * 128: gb2 * C + (kc + 1) * 128],
                        t_identb[:])
                return last
            tw = [xglw, w_setup]
            pj = (eix, j - 1) if j >= 1 else (eix - 1, 3)
            if pj in tok_xgev:
                tw.append(tok_xgev[pj])
            tok_xgtr[(eix, j)] = S("tensor", mk_xt, waits=tw)
            xdst = (t_xgT[:, (eix % 2) * NKC * 512:(eix % 2 + 1) * NKC * 512]
                    .rearrange("p (kc t) -> p kc t", kc=NKC)
                    [:, :, j * 128:(j + 1) * 128])
            ew = [tok_xgtr[(eix, j)]]
            if (eix - 2) in tok_hmm_last and j == 0:
                ew.append(tok_hmm_last[eix - 2])
            tok_xgev[(eix, j)] = S("vector", lambda e, xdst=xdst: e.tensor_copy(
                xdst, p_trB[:, 0:C].rearrange("p (kc t) -> p kc t", kc=NKC)),
                waits=ew)

        for iq in range(4):
            if eix == 0:
                slot_of_kc = {kc: pre_slot[(iq, kc)] for kc in range(NKC)}
            else:
                slot_of_kc = {}
                for kc in range(NKC):
                    L = load_ctr[0]
                    s = L % 32
                    nw_slot[s] += 1
                    slot_of_kc[kc] = (s, nw_slot[s])
                    lw = ([slot_last_reader[s]]
                          if slot_last_reader[s] is not None else [])
                    S("gpsimd", lambda e, s=s, eix=eix, kc=kc, iq=iq: e.dma_start(
                        t_ring[:, s * 1024:(s + 1) * 1024],
                        d_w1[eix, kc * 128:(kc + 1) * 128,
                             iq * 1024:(iq + 1) * 1024]).then_inc(q_slot[s], 16),
                      waits=lw)
                    load_ctr[0] += 1
            for icl in range(8):
                icg = iq * 8 + icl
                bank = p_i[icg % 4]

                def mk_im(eng, bank=bank, icl=icl, eix=eix,
                          slot_of_kc=dict(slot_of_kc)):
                    last = None
                    for kc in range(NKC):
                        s2, _ = slot_of_kc[kc]
                        last = eng.matmul(
                            bank[:],
                            t_ring[:, s2 * 1024 + icl * 128: s2 * 1024 + (icl + 1) * 128],
                            t_xgT[:, (eix % 2) * NKC * 512 + kc * 512:
                                  (eix % 2) * NKC * 512 + (kc + 1) * 512],
                            start=(kc == 0), stop=(kc == NKC - 1))
                    return last
                mw = [tok_xgev[(eix, 3)]]
                mw += [(q_slot[s2], 16 * n2) for (s2, n2) in slot_of_kc.values()]
                if (eix, icg - 4) in tok_gelu:
                    mw.append(tok_gelu[(eix, icg - 4)])
                elif eix >= 1 and icg < 4:
                    mw.append(tok_gelu[(eix - 1, 28 + icg)])
                tok_hmm[(eix, icg)] = S("tensor", mk_im, waits=mw)

                gw = [tok_hmm[(eix, icg)], w_b1c]
                if icg == 0 and (eix - 1) in tok_womm_last:
                    gw.append(tok_womm_last[eix - 1])
                tok_gelu[(eix, icg)] = S("scalar", lambda e, bank=bank, eix=eix,
                                         icg=icg:
                                         e.activation(
                                             t_zg[:, icg * 512:(icg + 1) * 512],
                                             bank[:], AF.Gelu,
                                             bias=t_b1c[:, icg * 8 + eix:
                                                        icg * 8 + eix + 1]),
                                         waits=gw)
            for kc in range(NKC):
                s2, _ = slot_of_kc[kc]
                slot_last_reader[s2] = tok_hmm[(eix, iq * 8 + 7)]
        tok_hmm_last[eix] = tok_hmm[(eix, 31)]
        tok_gelu_last[eix] = tok_gelu[(eix, 31)]

        wo_slot = {}
        for ic in range(32):
            L = load_ctr[0]
            s = L % 32
            nw_slot[s] += 1
            wo_slot[ic] = (s, nw_slot[s])
            lw = [slot_last_reader[s]] if slot_last_reader[s] is not None else []
            if ic == 0:
                lw.append((q_wob, 16))
            S("gpsimd", lambda e, s=s, ic=ic: e.dma_start(
                t_ring[:, s * 1024:(s + 1) * 1024],
                d_wob[ic * 128:(ic + 1) * 128, :]).then_inc(q_slot[s], 16),
              waits=lw)
            load_ctr[0] += 1

        for j in range(4):
            for ch2 in range(2):
                def mk_wo(eng, j=j, ch2=ch2, wo_slot=dict(wo_slot)):
                    last = None
                    for ic in range(32):
                        s2, _ = wo_slot[ic]
                        last = eng.matmul(
                            p_y2[ch2][:],
                            t_zg[:, ic * 512 + j * 128: ic * 512 + (j + 1) * 128],
                            t_ring[:, s2 * 1024 + ch2 * 512: s2 * 1024 + (ch2 + 1) * 512],
                            start=(ic == 0), stop=(ic == 31))
                    return last
                ww = [tok_gelu_last[eix]]
                ww += [(q_slot[s2], 16 * n2) for (s2, n2) in wo_slot.values()]
                if j >= 1:
                    ww.append(tok_yev[(eix, j - 1, ch2)])
                elif eix >= 1:
                    ww.append(tok_yev[(eix - 1, 3, ch2)])
                tok_womm[(eix, j, ch2)] = S("tensor", mk_wo, waits=ww)
                ew = [tok_womm[(eix, j, ch2)]]
                if eix >= 1:
                    ew.append((q_ygs[j], nyg[j]))
                tok_yev[(eix, j, ch2)] = S("vector", lambda e, j=j, ch2=ch2:
                                           e.tensor_copy(
                                               t_yst[:, j * C + ch2 * 512:
                                                     j * C + ch2 * 512 + 512],
                                               p_y2[ch2][:]), waits=ew)
        for ic in range(32):
            s2, _ = wo_slot[ic]
            slot_last_reader[s2] = tok_womm[(eix, 3, 1)]
        tok_womm_last[eix] = tok_womm[(eix, 3, 1)]
        for j in range(4):
            nyg[j] += 16
            S("sync", lambda e, eix=eix, j=j: e.dma_start(
                d_yg[eix * 512 + j * 128: eix * 512 + (j + 1) * 128, :],
                t_yst[:, j * C:(j + 1) * C]).then_inc(q_ygs[j], 16),
              waits=[tok_yev[(eix, j, 0)], tok_yev[(eix, j, 1)]])

    w_yall = [(q_ygs[j], nyg[j]) for j in range(4)]

    nygr = [0]
    tok_pair_res = None
    tok_pair_out = None
    for pr in range(8):
        pair = (2 * pr, 2 * pr + 1)
        gw = [tok_dsi]
        if pr == 0:
            gw += list(w_yall)
        if tok_pair_res is not None:
            gw.append(tok_pair_res)
        for half, tt2 in enumerate(pair):
            nygr[0] += 16
            S("gpsimd", lambda e, tt2=tt2, half=half: e.indirect_dma_start(
                t_yg[:, half * C:(half + 1) * C], None,
                d_yg, IndirectOffsetOnAxis(ap=t_dsi[:, tt2:tt2 + 1], axis=0),
                bounds_check=NE * 512 - 1, oob_is_err=False).then_inc(
                q_ygr[0], 16), waits=(gw if half == 0 else []))
        ygw = (q_ygr[0], nygr[0])
        rw = [(q_spill, 16 * NTT)]
        if tok_pair_res is not None:
            rw.append(tok_pair_res)
        for half, tt2 in enumerate(pair):
            n_xrel2[half] += 1
            S("sync", lambda e, tt2=tt2, half=half: e.dma_start(
                t_xrel2[:, half * C:(half + 1) * C],
                d_x1[tt2 * 128:(tt2 + 1) * 128, :]).then_inc(
                q_xrel2[half], 16), waits=(rw if half == 0 else []))
        xr2w = [(q_xrel2[0], 16 * n_xrel2[0]), (q_xrel2[1], 16 * n_xrel2[1])]

        Y, XR, CT, YG = t_y2, t_xrel2, t_ct2, t_yg
        sc = t_st2
        Y3 = Y[:].rearrange("p (h c) -> p h c", h=2)
        XR3 = XR[:].rearrange("p (h c) -> p h c", h=2)
        CT3 = CT[:].rearrange("p (h c) -> p h c", h=2)
        YG3 = YG[:].rearrange("p (h c) -> p h c", h=2)
        bo2 = t_bo[:].rearrange("p c -> p () c").broadcast_to([128, 2, C])
        gam2 = t_gam[:].rearrange("p c -> p () c").broadcast_to([128, 2, C])
        bet2 = t_bet[:].rearrange("p c -> p () c").broadcast_to([128, 2, C])

        r0w = [ygw, w_setup] + xr2w
        if tok_pair_out is not None:
            r0w += tok_pair_out
        t_a = S("vector", lambda e: e.tensor_tensor(Y3, YG3, bo2, op=ALU.add),
                waits=r0w)
        t_r1 = S("vector", lambda e: e.tensor_tensor(Y[:], Y[:], XR[:],
                                                     op=ALU.add), waits=[t_a])

        def ln_pair(first_w, eps, affine, outp):
            tns = S("vector", lambda e: e.reduce_sum(
                sc[:, 0:2].rearrange("p h -> p h ()"), Y3, axis=AX.X,
                negate=True), waits=first_w)
            tnm = S("vector", lambda e: e.tensor_scalar(
                sc[:, 2:4], sc[:, 0:2], 1.0 / C, None, op0=ALU.mult), waits=[tns])
            tct = None
            for hh in range(2):
                tct = S("scalar", lambda e, hh=hh: e.activation(
                    CT[:, hh * C:(hh + 1) * C], Y[:, hh * C:(hh + 1) * C],
                    AF.Identity, bias=sc[:, 2 + hh:3 + hh]),
                    waits=[tnm] if hh == 0 else [])
            tsq = S("scalar", lambda e: e.activation(Y[:], CT[:], AF.Square),
                    waits=[tct])
            tvr = S("vector", lambda e: e.reduce_sum(
                sc[:, 4:6].rearrange("p h -> p h ()"), Y3, axis=AX.X),
                waits=[tsq])
            tv2 = S("vector", lambda e, eps=eps: e.tensor_scalar(
                sc[:, 6:8], sc[:, 4:6], 1.0 / C, eps, op0=ALU.mult,
                op1=ALU.add), waits=[tvr])
            tln = S("scalar", lambda e: e.activation(sc[:, 8:10], sc[:, 6:8],
                                                     AF.Ln), waits=[tv2])
            trs = S("scalar", lambda e: e.activation(sc[:, 10:12], sc[:, 8:10],
                                                     AF.Exp, scale=-0.5),
                    waits=[tln])
            if affine:
                tz = None
                for hh in range(2):
                    tz = S("scalar", lambda e, hh=hh: e.activation(
                        Y[:, hh * C:(hh + 1) * C], CT[:, hh * C:(hh + 1) * C],
                        AF.Identity, scale=sc[:, 10 + hh:11 + hh]),
                        waits=[trs] if hh == 0 else [])
                tz = S("vector", lambda e: e.tensor_tensor(Y3, Y3, gam2,
                                                           op=ALU.mult), waits=[tz])
                tz = S("vector", lambda e: e.tensor_tensor(Y3, Y3, bet2,
                                                           op=ALU.add), waits=[tz])
            else:
                tz = None
                for hh in range(2):
                    tz = S("scalar", lambda e, hh=hh: e.activation(
                        t_ct2[:, hh * C:(hh + 1) * C],
                        CT[:, hh * C:(hh + 1) * C],
                        AF.Identity, scale=sc[:, 10 + hh:11 + hh]),
                        waits=[trs] if hh == 0 else [])
            return tz

        t_z = ln_pair([t_r1], EPS2, True, None)
        t_r2 = S("vector", lambda e: e.tensor_tensor(Y[:], XR[:], Y[:],
                                                     op=ALU.add), waits=[t_z])
        tok_pair_res = t_r2
        t_o = ln_pair([t_r2], EPS1, False, CT3)

        tok_pair_out = []
        for half, tt2 in enumerate(pair):
            n_out2[half] += 16
            S("sync", lambda e, tt2=tt2, half=half: e.dma_start(
                d_out[tt2 * 128:(tt2 + 1) * 128, :],
                t_ct2[:, half * C:(half + 1) * C]).then_inc(q_out2[half], 16),
              waits=[t_o])
            tok_pair_out.append((q_out2[half], n_out2[half]))

    S("sync", lambda e: e.nop(), waits=[(q_out2[0], n_out2[0]), (q_out2[1], n_out2[1])])

    sched.finalize()
    for blk_ops in sched.blocks:
        with nc.Block() as block:
            for ename, deco in (("sync", block.sync), ("gpsimd", block.gpsimd),
                                ("scalar", block.scalar), ("vector", block.vector),
                                ("tensor", block.tensor)):
                def body(engine, ename=ename, blk_ops=blk_ops):
                    sched.emit_block(blk_ops, ename, engine, eng_sems)
                deco(body)

    b_st.close()
    st.close()
    return nc


_NC_CACHE = {}


def _get_nc():
    if "nc" not in _NC_CACHE:
        _NC_CACHE["nc"] = build_program()
    return _NC_CACHE["nc"]


def _build_masks(seq0):
    p = np.arange(128)[:, None]
    f = np.arange(W3)[None, :]
    out = np.zeros((3, 128, W3), dtype=np.float32)
    for v in range(3):
        tt = v if v < 2 else 2
        allowed = (f >= p) & (f <= p + 256)
        if v < 2:
            kseq = seq0 - 256 + tt * 128 + f
            allowed = allowed & (kseq >= 0)
        out[v] = np.where(allowed, 0.0, NEG)
    return out


def kernel(L0_attn_x, L0_attn_W_attn, L0_attn_b_attn, L0_attn_W_proj, L0_attn_b_proj,
           L0_ffn_Wr, L0_ffn_W1, L0_ffn_b1, L0_ffn_Wo, L0_ffn_bo, L0_ffn_gamma,
           L0_ffn_beta):
    x = np.ascontiguousarray(np.asarray(L0_attn_x, dtype=np.float32).reshape(B * T, C))
    wa = np.ascontiguousarray(np.asarray(L0_attn_W_attn, dtype=np.float32))
    ba = np.ascontiguousarray(np.asarray(L0_attn_b_attn, dtype=np.float32))
    wp = np.ascontiguousarray(np.asarray(L0_attn_W_proj, dtype=np.float32))
    bp = np.ascontiguousarray(np.asarray(L0_attn_b_proj, dtype=np.float32))
    wrt = np.ascontiguousarray(np.asarray(L0_ffn_Wr, dtype=np.float32).T)
    w1 = np.ascontiguousarray(np.asarray(L0_ffn_W1, dtype=np.float32))
    b1 = np.ascontiguousarray(np.asarray(L0_ffn_b1, dtype=np.float32))
    wo = np.ascontiguousarray(np.asarray(L0_ffn_Wo, dtype=np.float32))
    bo = np.ascontiguousarray(np.asarray(L0_ffn_bo, dtype=np.float32))
    gam = np.ascontiguousarray(np.asarray(L0_ffn_gamma, dtype=np.float32))
    bet = np.ascontiguousarray(np.asarray(L0_ffn_beta, dtype=np.float32))

    in_maps = []
    for c in range(NCORES):
        base = c * TCORE
        seq0 = (c % 4) * TCORE
        xe = np.zeros((NTQ * 128, C), dtype=np.float32)
        if seq0 == 0:
            xe[HALO:] = x[base:base + TCORE]
        else:
            xe[:] = x[base - HALO:base + TCORE]
        in_maps.append({
            "x_ext": xe, "w_attn": wa, "b_attn": ba, "w_proj": wp, "b_proj": bp,
            "wr_t": wrt, "w1": w1, "b1": b1, "wo": wo, "bo": bo, "gamma": gam,
            "beta": bet, "masks": _build_masks(seq0),
            "eids": np.arange(NE, dtype=np.float32).reshape(NE, 1),
            "ones8": np.ones((NE, 1), dtype=np.float32),
        })

    nc = _get_nc()
    res = run_bass_kernel_spmd(nc, in_maps, core_ids=list(range(NCORES)))
    out = np.concatenate([r["out"] for r in res.results], axis=0)
    return out.reshape(B, T, C)
